# version 01
# baseline (speedup 1.0000x reference)
import numpy as np
from contextlib import ExitStack

import concourse.bass as bass
import concourse.mybir as mybir
from concourse.bass_utils import run_bass_kernel_spmd

F32 = mybir.dt.float32
U32 = mybir.dt.uint32
AF = mybir.ActivationFunctionType
OP = mybir.AluOpType
AX = mybir.AxisListType

NCORES = 8
T = 16384
D = 1024
TS = T // NCORES
NTILE = TS // 128



class P:
    def __init__(self, name):
        self.nc = bass.Bass("TRN2", target_bir_lowering=False)
        self.name = name
        self.stack = ExitStack()
        self.sem = self.stack.enter_context(self.nc.semaphore())
        self.steps = []
        self.outs = []
        self.built = False

    def din(self, name, shape, dtype=F32):
        return self.nc.dram_tensor(name, list(shape), dtype, kind="ExternalInput").ap()

    def dout(self, name, shape, dtype=F32):
        self.outs.append(name)
        return self.nc.dram_tensor(name, list(shape), dtype, kind="ExternalOutput").ap()

    def sb(self, name, shape, dtype=F32):
        return self.stack.enter_context(self.nc.sbuf_tensor(name, list(shape), dtype))

    def ps(self, name, shape, dtype=F32):
        return self.stack.enter_context(self.nc.psum_tensor(name, list(shape), dtype))

    def step(self, eng, fn, inc=1):
        self.steps.append((eng, fn, inc))

    def load(self, out, in_):
        self.step("sync", lambda e, o=out, i=in_: e.dma_start(out=o, in_=i), 16)

    def store(self, out, in_):
        self.step("gpsimd", lambda e, o=out, i=in_: e.dma_start(out=o, in_=i), 16)

    def build(self):
        if self.built:
            return self
        nc = self.nc
        pre = []
        v = 0
        for (_, _, inc) in self.steps:
            pre.append(v)
            v += inc
        total = v
        sem = self.sem
        steps = self.steps
        with nc.Block() as block:
            def mk(engname):
                def body(eng):
                    for (e, fn, inc), pv in zip(steps, pre):
                        if e == engname:
                            eng.wait_ge(sem, pv)
                            inst = fn(eng)
                            if inc:
                                inst.then_inc(sem, inc)
                    eng.wait_ge(sem, total)
                return body
            block.sync(mk("sync"))
            block.gpsimd(mk("gpsimd"))
            block.vector(mk("vector"))
            block.scalar(mk("scalar"))
            block.tensor(mk("tensor"))
        self.built = True
        return self


_progs = {}


def get_prog(name, builder):
    if name not in _progs:
        _progs[name] = builder().build()
    return _progs[name]


def launch(p, inmaps):
    res = run_bass_kernel_spmd(p.nc, inmaps, core_ids=list(range(NCORES)))
    return [{k: np.asarray(r[k]) for k in p.outs} for r in res.results]



def build_lngb():
    p = P("lngb")
    xin = p.din("xin", (TS, D))
    gin = p.din("g", (128, D))
    bin_ = p.din("b", (128, D))
    y = p.dout("y", (TS, D))
    xt = p.sb("xt", (128, D))
    gt = p.sb("gt", (128, D))
    bt = p.sb("bt", (128, D))
    sq = p.sb("sq", (128, D))
    yt = p.sb("yt", (128, D))
    s1 = p.sb("s1", (128, 1))
    s2 = p.sb("s2", (128, 1))
    s3 = p.sb("s3", (128, 1))
    p.load(gt[:, :], gin[:, :])
    p.load(bt[:, :], bin_[:, :])
    for t in range(NTILE):
        p.load(xt[:, :], xin[t * 128:(t + 1) * 128, :])
        p.step("vector", lambda e: e.tensor_reduce(s1[:, :], xt[:, :], axis=AX.X, op=OP.add))
        p.step("scalar", lambda e: e.mul(s2[:, :], s1[:, :], -1.0 / D))
        p.step("vector", lambda e: e.tensor_scalar(xt[:, :], xt[:, :], s2[:, 0:1], None, op0=OP.add))
        p.step("scalar", lambda e: e.activation(sq[:, :], xt[:, :], AF.Square))
        p.step("vector", lambda e: e.tensor_reduce(s1[:, :], sq[:, :], axis=AX.X, op=OP.add))
        p.step("scalar", lambda e: e.mul(s2[:, :], s1[:, :], 1.0 / D))
        p.step("vector", lambda e: e.tensor_scalar_add(s2[:, :], s2[:, :], 1e-5))
        p.step("scalar", lambda e: e.activation(s3[:, :], s2[:, :], AF.Sqrt))
        p.step("vector", lambda e: e.reciprocal(s1[:, :], s3[:, :]))
        p.step("vector", lambda e: e.tensor_scalar(xt[:, :], xt[:, :], s1[:, 0:1], None, op0=OP.mult))
        p.step("vector", lambda e: e.tensor_tensor(yt[:, :], xt[:, :], gt[:, :], op=OP.mult))
        p.step("vector", lambda e: e.tensor_tensor(yt[:, :], yt[:, :], bt[:, :], op=OP.add))
        p.store(y[t * 128:(t + 1) * 128, :], yt[:, :])
    return p



def build_add3():
    p = P("add3")
    a = p.din("a", (TS, D))
    b = p.din("b", (TS, D))
    c = p.din("c", (TS, D))
    y = p.dout("y", (TS, D))
    at = p.sb("at", (128, D))
    bt = p.sb("bt", (128, D))
    ct = p.sb("ct", (128, D))
    for t in range(NTILE):
        sl = slice(t * 128, (t + 1) * 128)
        p.load(at[:, :], a[sl, :])
        p.load(bt[:, :], b[sl, :])
        p.load(ct[:, :], c[sl, :])
        p.step("vector", lambda e: e.tensor_tensor(at[:, :], at[:, :], bt[:, :], op=OP.add))
        p.step("vector", lambda e: e.tensor_tensor(at[:, :], at[:, :], ct[:, :], op=OP.add))
        p.store(y[sl, :], at[:, :])
    return p



def build_mm(name, K, N):
    p = P(name)
    kt = K // 128
    nt = N // 128
    a = p.din("aT", (K, TS))
    w = p.din("w", (K, N))
    bias = p.din("bias", (128, nt))
    y = p.dout("yT", (N, TS))
    wsb = p.sb("wsb", (128, kt * N))
    asb = p.sb("asb", (128, kt * 512))
    bsb = p.sb("bsb", (128, nt))
    ysb = p.sb("ysb", (128, 512))
    psum = p.ps("psum", (128, 512))
    p.load(bsb[:, :], bias[:, :])
    for dt in range(kt):
        p.load(wsb[:, dt * N:(dt + 1) * N], w[dt * 128:(dt + 1) * 128, :])
    for tb in range(TS // 512):
        for dt in range(kt):
            p.load(asb[:, dt * 512:(dt + 1) * 512],
                   a[dt * 128:(dt + 1) * 128, tb * 512:(tb + 1) * 512])
        for ot in range(nt):
            for dt in range(kt):
                p.step("tensor", lambda e, o=ot, d=dt: e.matmul(
                    psum[:, :],
                    wsb[:, d * N + o * 128:d * N + (o + 1) * 128],
                    asb[:, d * 512:(d + 1) * 512],
                    start=(d == 0), stop=(d == kt - 1)))
            p.step("vector", lambda e, o=ot: e.tensor_scalar(
                ysb[:, :], psum[:, :], bsb[:, o:o + 1], None, op0=OP.add))
            p.store(y[ot * 128:(ot + 1) * 128, tb * 512:(tb + 1) * 512], ysb[:, :])
    return p



def build_phi():
    p = P("phi")
    a = p.din("a", (D, TS))
    b = p.din("b", (D, TS))
    ya = p.dout("ya", (D, TS))
    yb = p.dout("yb", (D, TS))
    xt = p.sb("xt", (128, TS))
    t1 = p.sb("t1", (128, TS))
    t2 = p.sb("t2", (128, TS))
    for (src, dst) in ((a, ya), (b, yb)):
        for ft in range(D // 128):
            sl = slice(ft * 128, (ft + 1) * 128)
            p.load(xt[:, :], src[sl, :])
            p.step("vector", lambda e: e.tensor_scalar_min(t1[:, :], xt[:, :], 0.0))
            p.step("scalar", lambda e: e.activation(t2[:, :], t1[:, :], AF.Exp))
            p.step("vector", lambda e: e.tensor_scalar_max(t1[:, :], xt[:, :], 0.0))
            p.step("vector", lambda e: e.tensor_tensor(xt[:, :], t1[:, :], t2[:, :], op=OP.add))
            p.store(dst[sl, :], xt[:, :])
    return p



def build_scan():
    p = P("scan")
    S = 4096
    C = 128
    NCH = S // C
    qT = p.din("qT", (8, 64, S))
    kT = p.din("kT", (8, 64, S))
    ktok = p.din("ktok", (8, S, 64))
    vaug = p.din("vaug", (8, S, 65))
    maskT = p.din("maskT", (128, 128))
    o = p.dout("o", (8, S, 64))
    qsb = p.sb("qsb", (64, S))
    ksb = p.sb("ksb", (64, S))
    ktsb = p.sb("ktsb", (128, NCH * 64))
    vsb = p.sb("vsb", (128, NCH * 65))
    msb = p.sb("msb", (128, 128))
    ssb = p.sb("ssb", (128, 128))
    Ssb = p.sb("Ssb", (64, 65))
    dsb = p.sb("dsb", (128, 1))
    osb = p.sb("osb", (128, 64))
    ps_s = p.ps("ps_s", (128, 128))
    ps_n = p.ps("ps_n", (128, 65))
    ps_S = p.ps("ps_S", (64, 65))
    p.load(msb[:, :], maskT[:, :])
    for pr in range(8):
        p.load(qsb[:, :], qT[pr, :, :])
        p.load(ksb[:, :], kT[pr, :, :])
        p.load(ktsb[:, :].rearrange("p (c d) -> p c d", d=64),
               ktok[pr, :, :].rearrange("(c p) d -> p c d", p=128))
        p.load(vsb[:, :].rearrange("p (c d) -> p c d", d=65),
               vaug[pr, :, :].rearrange("(c p) d -> p c d", p=128))
        p.step("vector", lambda e: e.memset(Ssb[:, :], 0.0))
        for c in range(NCH):
            cs = slice(c * 128, (c + 1) * 128)
            p.step("tensor", lambda e, cs=cs: e.matmul(
                ps_s[:, :], ksb[:, cs], qsb[:, cs], start=True, stop=True))
            p.step("vector", lambda e: e.tensor_tensor(ssb[:, :], ps_s[:, :], msb[:, :], op=OP.mult))
            p.step("tensor", lambda e, c=c: e.matmul(
                ps_n[:, :], ssb[:, :], vsb[:, c * 65:(c + 1) * 65], start=True, stop=False))
            p.step("tensor", lambda e, cs=cs: e.matmul(
                ps_n[:, :], qsb[:, cs], Ssb[:, :], start=False, stop=True))
            p.step("tensor", lambda e, c=c: e.matmul(
                ps_S[:, :], ktsb[:, c * 64:(c + 1) * 64], vsb[:, c * 65:(c + 1) * 65],
                start=(c == 0), stop=(c == NCH - 1)))
            p.step("vector", lambda e: e.tensor_copy(Ssb[:, :], ps_S[:, :]))
            p.step("vector", lambda e: e.reciprocal(dsb[:, :], ps_n[:, 64:65]))
            p.step("vector", lambda e: e.tensor_scalar(
                osb[:, :], ps_n[:, 0:64], dsb[:, 0:1], None, op0=OP.mult))
            p.store(o[pr, c * 128:(c + 1) * 128, :], osb[:, :])
    return p



def build_att1():
    p = P("att1")
    S = 4096
    qT = p.din("qT", (8, 64, S))
    kT = p.din("kT", (8, 64, S))
    vaug = p.din("vaug", (8, S, 65))
    maskT = p.din("maskT", (128, 128))
    ident = p.din("ident", (128, 128))
    o = p.dout("o", (8, 64, S))
    qsb = p.sb("qsb", (64, S))
    ksb = p.sb("ksb", (64, S))
    vsb = p.sb("vsb", (128, 32 * 65))
    msb = p.sb("msb", (128, 128))
    isb = p.sb("isb", (128, 128))
    ssb = p.sb("ssb", (128, 512))
    csb = p.sb("csb", (65, 512))
    one1 = p.sb("one1", (1, 128))
    dsb = p.sb("dsb", (1, 512))
    bsb = p.sb("bsb", (128, 512))
    osb = p.sb("osb", (64, 512))
    ps_s = p.ps("ps_s", (128, 512))
    ps_c = p.ps("ps_c", (65, 512))
    ps_b = p.ps("ps_b", (128, 512))
    p.load(msb[:, :], maskT[:, :])
    p.load(isb[:, :], ident[:, :])
    p.step("vector", lambda e: e.memset(one1[:, :], 1.0))
    for pr in range(8):
        p.load(qsb[:, :], qT[pr, :, :])
        p.load(ksb[:, :], kT[pr, :, :])
        p.load(vsb[:, :].rearrange("p (c d) -> p c d", d=65),
               vaug[pr, :, :].rearrange("(c p) d -> p c d", p=128))
        for g in range(8):
            gsl = slice(g * 512, (g + 1) * 512)
            for j in range(4 * g + 4):
                p.step("tensor", lambda e, j=j, gsl=gsl: e.matmul(
                    ps_s[:, :], ksb[:, j * 128:(j + 1) * 128], qsb[:, gsl],
                    start=True, stop=True))
                p.step("scalar", lambda e: e.activation(ssb[:, :], ps_s[:, :], AF.Relu, scale=0.125))
                if j >= 4 * g:
                    jj = j - 4 * g
                    if jj > 0:
                        p.step("vector", lambda e, jj=jj: e.memset(ssb[:, :jj * 128], 0.0))
                    p.step("vector", lambda e, jj=jj: e.tensor_tensor(
                        ssb[:, jj * 128:(jj + 1) * 128], ssb[:, jj * 128:(jj + 1) * 128],
                        msb[:, :], op=OP.mult))
                p.step("tensor", lambda e, j=j: e.matmul(
                    ps_c[:, :], vsb[:, j * 65:(j + 1) * 65], ssb[:, :],
                    start=(j == 0), stop=(j == 4 * g + 3)))
            p.step("vector", lambda e: e.tensor_copy(csb[:, :], ps_c[:, :]))
            p.step("vector", lambda e: e.tensor_scalar_add(dsb[:, :], csb[64:65, :], 1e-9))
            p.step("vector", lambda e: e.reciprocal(dsb[:, :], dsb[:, :]))
            p.step("tensor", lambda e: e.matmul(
                ps_b[:, :], one1[:, :], dsb[:, :], start=True, stop=True))
            p.step("vector", lambda e: e.tensor_copy(bsb[:, :], ps_b[:, :]))
            p.step("vector", lambda e: e.tensor_tensor(
                osb[:, :], csb[0:64, :], bsb[0:64, :], op=OP.mult))
            p.store(o[pr, :, gsl], osb[:, :])
    return p



def build_rot():
    p = P("rot")
    q = p.din("q", (TS, D))
    k = p.din("k", (TS, D))
    qs = p.din("qs", (128, D))
    ks = p.din("ks", (128, D))
    cs = p.din("cs", (TS, D))
    ss = p.din("ss", (TS, D))
    ci = p.din("ci", (TS, D))
    si = p.din("si", (TS, D))
    qr = p.dout("qr", (TS, D))
    kr = p.dout("kr", (TS, D))
    xt = p.sb("xt", (128, D))
    sct = p.sb("sct", (128, D))
    ct = p.sb("ct", (128, D))
    st = p.sb("st", (128, D))
    sq = p.sb("sq", (128, D))
    rh = p.sb("rh", (128, D))
    t1 = p.sb("t1", (128, D))
    nrm = p.sb("nrm", (128, 16))
    for (src, scl, cosin, dst, qscale) in ((q, qs, (cs, ss), qr, 8.0), (k, ks, (ci, si), kr, None)):
        p.load(sct[:, :], scl[:, :])
        for t in range(NTILE):
            sl = slice(t * 128, (t + 1) * 128)
            p.load(xt[:, :], src[sl, :])
            p.load(ct[:, :], cosin[0][sl, :])
            p.load(st[:, :], cosin[1][sl, :])
            p.step("scalar", lambda e: e.activation(sq[:, :], xt[:, :], AF.Square))
            for h in range(16):
                p.step("vector", lambda e, h=h: e.tensor_reduce(
                    nrm[:, h:h + 1], sq[:, h * 64:(h + 1) * 64], axis=AX.X, op=OP.add))
            p.step("scalar", lambda e: e.activation(nrm[:, :], nrm[:, :], AF.Sqrt))
            p.step("vector", lambda e: e.tensor_scalar_max(nrm[:, :], nrm[:, :], 1e-12))
            p.step("vector", lambda e: e.reciprocal(nrm[:, :], nrm[:, :]))
            for h in range(16):
                p.step("vector", lambda e, h=h: e.tensor_scalar(
                    xt[:, h * 64:(h + 1) * 64], xt[:, h * 64:(h + 1) * 64],
                    nrm[:, h:h + 1], None, op0=OP.mult))
            p.step("vector", lambda e: e.tensor_tensor(xt[:, :], xt[:, :], sct[:, :], op=OP.mult))
            xv = xt[:, :].rearrange("p (h u w) -> p h u w", h=16, u=2)
            rv = rh[:, :].rearrange("p (h u w) -> p h u w", h=16, u=2)
            p.step("scalar", lambda e: e.mul(rv[:, :, 0, :], xv[:, :, 1, :], -1.0))
            p.step("scalar", lambda e: e.copy(rv[:, :, 1, :], xv[:, :, 0, :]))
            p.step("vector", lambda e: e.tensor_tensor(t1[:, :], xt[:, :], ct[:, :], op=OP.mult))
            p.step("vector", lambda e: e.tensor_tensor(rh[:, :], rh[:, :], st[:, :], op=OP.mult))
            p.step("vector", lambda e: e.tensor_tensor(t1[:, :], t1[:, :], rh[:, :], op=OP.add))
            if qscale is not None:
                p.step("vector", lambda e, qs_=qscale: e.tensor_scalar_mul(t1[:, :], t1[:, :], qs_))
            p.store(dst[sl, :], t1[:, :])
    return p



def build_latt():
    p = P("latt")
    S = 4096
    W = 256
    NW = S // W
    qT = p.din("qT", (8, 64, S))
    kT = p.din("kT", (8, 64, S))
    vtok = p.din("vtok", (8, S, 64))
    m0 = p.din("m0", (256, 256))
    mw = p.din("mw", (256, 512))
    ident = p.din("ident", (128, 128))
    o = p.dout("o", (8, 64, S))
    qsb = p.sb("qsb", (64, S))
    ksb = p.sb("ksb", (64, S))
    vsb = p.sb("vsb", (128, 32 * 64))
    m0sb = p.sb("m0sb", (128, 2 * 256))
    mwsb = p.sb("mwsb", (128, 2 * 512))
    isb = p.sb("isb", (128, 128))
    es = p.sb("es", (128, 512))
    asb = p.sb("asb", (128, 512))
    atT = p.sb("atT", (128, 4 * 128))
    osb = p.sb("osb", (64, 128))
    r1 = p.sb("r1", (128, 1))
    r2 = p.sb("r2", (128, 1))
    ps_s = p.ps("ps_s", (128, 512))
    ps_t = p.ps("ps_t", (128, 128))
    ps_c = p.ps("ps_c", (64, 128))
    for s in range(2):
        p.load(m0sb[:, s * 256:(s + 1) * 256], m0[s * 128:(s + 1) * 128, :])
        p.load(mwsb[:, s * 512:(s + 1) * 512], mw[s * 128:(s + 1) * 128, :])
    p.load(isb[:, :], ident[:, :])
    for pr in range(8):
        p.load(qsb[:, :], qT[pr, :, :])
        p.load(ksb[:, :], kT[pr, :, :])
        p.load(vsb[:, :].rearrange("p (c d) -> p c d", d=64),
               vtok[pr, :, :].rearrange("(c p) d -> p c d", p=128))
        for w in range(NW):
            kbase = max(0, (w - 1) * W)
            kw = 256 if w == 0 else 512
            nkt = kw // 128
            for s in range(2):
                tq0 = w * W + s * 128
                msb = m0sb[:, s * 256:(s + 1) * 256] if w == 0 else mwsb[:, s * 512:(s + 1) * 512]
                p.step("tensor", lambda e, tq0=tq0, kbase=kbase, kw=kw: e.matmul(
                    ps_s[:, :kw], qsb[:, tq0:tq0 + 128], ksb[:, kbase:kbase + kw],
                    start=True, stop=True))
                p.step("vector", lambda e, kw=kw, msb=msb: e.tensor_tensor(
                    es[:, :kw], ps_s[:, :kw], msb, op=OP.add))
                p.step("vector", lambda e, kw=kw: e.tensor_reduce(
                    r1[:, :], es[:, :kw], axis=AX.X, op=OP.max))
                p.step("scalar", lambda e: e.mul(r2[:, :], r1[:, :], -1.0))
                p.step("scalar", lambda e, kw=kw: e.activation(
                    asb[:, :kw], es[:, :kw], AF.Exp, bias=r2[:, 0:1]))
                p.step("vector", lambda e, kw=kw: e.tensor_reduce(
                    r1[:, :], asb[:, :kw], axis=AX.X, op=OP.add))
                p.step("vector", lambda e: e.reciprocal(r2[:, :], r1[:, :]))
                p.step("vector", lambda e, kw=kw: e.tensor_scalar(
                    asb[:, :kw], asb[:, :kw], r2[:, 0:1], None, op0=OP.mult))
                for kt in range(nkt):
                    p.step("tensor", lambda e, kt=kt: e.transpose(
                        ps_t[:, :], asb[:, kt * 128:(kt + 1) * 128], isb[:, :]))
                    p.step("vector", lambda e, kt=kt: e.tensor_copy(
                        atT[:, kt * 128:(kt + 1) * 128], ps_t[:, :]))
                for kt in range(nkt):
                    vt = kbase // 128 + kt
                    p.step("tensor", lambda e, kt=kt, vt=vt, nkt=nkt: e.matmul(
                        ps_c[:, :], vsb[:, vt * 64:(vt + 1) * 64],
                        atT[:, kt * 128:(kt + 1) * 128],
                        start=(kt == 0), stop=(kt == nkt - 1)))
                p.step("vector", lambda e: e.tensor_copy(osb[:, :], ps_c[:, :]))
                p.store(o[pr, :, tq0:tq0 + 128], osb[:, :])
    return p



def build_ffnsub():
    p = P("ffnsub")
    N1 = 5632
    NI = 2816
    a = p.din("aT", (D, TS))
    w1 = p.din("w1", (D, N1))
    b1 = p.din("b1", (128, N1 // 128))
    w2 = p.din("w2", (NI, D))
    b2 = p.din("b2", (128, D // 128))
    y = p.dout("yT", (D, TS))
    asb = p.sb("asb", (128, 8 * 512))
    w1sb = p.sb("w1sb", (128, 8 * 128))
    w2sb = p.sb("w2sb", (128, 22 * 128))
    b1sb = p.sb("b1sb", (128, N1 // 128))
    b2sb = p.sb("b2sb", (128, D // 128))
    ag = p.sb("ag", (128, 22 * 512))
    gt = p.sb("gt", (128, 512))
    ysb = p.sb("ysb", (128, 512))
    psum = p.ps("psum", (128, 512))
    ps2 = p.ps("ps2", (128, 512))
    p.load(b1sb[:, :], b1[:, :])
    p.load(b2sb[:, :], b2[:, :])
    nit = NI // 128
    for tb in range(TS // 512):
        for dt in range(8):
            p.load(asb[:, dt * 512:(dt + 1) * 512],
                   a[dt * 128:(dt + 1) * 128, tb * 512:(tb + 1) * 512])
        for ft in range(2 * nit):
            p.load(w1sb[:, :].rearrange("p (dt o) -> p dt o", o=128),
                   w1[:, ft * 128:(ft + 1) * 128].rearrange("(dt p) o -> p dt o", p=128))
            for dt in range(8):
                p.step("tensor", lambda e, d=dt: e.matmul(
                    psum[:, :], w1sb[:, d * 128:(d + 1) * 128], asb[:, d * 512:(d + 1) * 512],
                    start=(d == 0), stop=(d == 7)))
            if ft < nit:
                p.step("vector", lambda e, ft=ft: e.tensor_scalar(
                    ag[:, ft * 512:(ft + 1) * 512], psum[:, :], b1sb[:, ft:ft + 1],
                    None, op0=OP.add))
            else:
                p.step("scalar", lambda e, ft=ft: e.activation(
                    gt[:, :], psum[:, :], AF.Gelu, bias=b1sb[:, ft:ft + 1]))
                p.step("vector", lambda e, ft=ft: e.tensor_tensor(
                    ag[:, (ft - nit) * 512:(ft - nit + 1) * 512],
                    ag[:, (ft - nit) * 512:(ft - nit + 1) * 512], gt[:, :], op=OP.mult))
        for ot in range(8):
            p.load(w2sb[:, :].rearrange("p (it o) -> p it o", o=128),
                   w2[:, ot * 128:(ot + 1) * 128].rearrange("(it p) o -> p it o", p=128))
            for it in range(nit):
                p.step("tensor", lambda e, it=it: e.matmul(
                    ps2[:, :], w2sb[:, it * 128:(it + 1) * 128], ag[:, it * 512:(it + 1) * 512],
                    start=(it == 0), stop=(it == nit - 1)))
            p.step("vector", lambda e, ot=ot: e.tensor_scalar(
                ysb[:, :], ps2[:, :], b2sb[:, ot:ot + 1], None, op0=OP.add))
            p.store(y[ot * 128:(ot + 1) * 128, tb * 512:(tb + 1) * 512], ysb[:, :])
    return p



def build_ffn0h():
    p = P("ffn0h")
    xnT = p.din("xnT", (D, T))
    wq = p.din("wq", (D, 256))
    gam = p.din("gam", (128, 2))
    bet = p.din("bet", (128, 2))
    keysT = p.din("keysT", (2, 128, 256))
    ids = p.dout("ids", (T, 8))
    w8o = p.dout("w8", (T, 8))
    q2 = p.sb("q2", (128, 2 * T))
    xsb = p.sb("xsb", (128, 8 * 512))
    wqsb = p.sb("wqsb", (128, 8 * 256))
    ksb = p.sb("ksb", (128, 512))
    sq = p.sb("sq", (128, 512))
    accs = p.sb("accs", (128, 2))
    accq = p.sb("accq", (128, 2))
    tmp1 = p.sb("tmp1", (128, 1))
    g2 = p.sb("g2", (128, 2))
    b2 = p.sb("b2", (128, 2))
    gamsb = p.sb("gamsb", (128, 2))
    betsb = p.sb("betsb", (128, 2))
    simsb = p.sb("simsb", (128, 256))
    sx = p.sb("sx", (128, 8))
    sy = p.sb("sy", (128, 8))
    ixu = p.sb("ixu", (128, 8), U32)
    iyu = p.sb("iyu", (128, 8), U32)
    ixf = p.sb("ixf", (128, 8))
    iyf = p.sb("iyf", (128, 8))
    asc = p.sb("asc", (128, 64))
    aid = p.sb("aid", (128, 64))
    s8 = p.sb("s8", (128, 8))
    e8 = p.sb("e8", (128, 8))
    w8 = p.sb("w8sb", (128, 8))
    idsel = p.sb("idsel", (128, 8))
    ohsb = p.sb("ohsb", (128, 64))
    junk = p.sb("junk", (128, 64))
    psum = p.ps("psum", (128, 512))
    ps_sim = p.ps("ps_sim", (128, 256))
    p.load(wqsb[:, :].rearrange("p (dt o) -> p dt o", o=256),
           wq[:, :].rearrange("(dt p) o -> p dt o", p=128))
    p.load(ksb[:, 0:256], keysT[0, :, :])
    p.load(ksb[:, 256:512], keysT[1, :, :])
    p.load(gamsb[:, :], gam[:, :])
    p.load(betsb[:, :], bet[:, :])
    p.step("vector", lambda e: e.memset(accs[:, :], 0.0))
    p.step("vector", lambda e: e.memset(accq[:, :], 0.0))
    for tb in range(T // 512):
        p.load(xsb[:, :].rearrange("p (dt t) -> p dt t", t=512),
               xnT[:, tb * 512:(tb + 1) * 512].rearrange("(dt p) t -> p dt t", p=128))
        for f in range(2):
            for dt in range(8):
                p.step("tensor", lambda e, f=f, d=dt: e.matmul(
                    psum[:, :], wqsb[:, d * 256 + f * 128:d * 256 + (f + 1) * 128],
                    asb_slice(xsb, d), start=(d == 0), stop=(d == 7)))
            p.step("vector", lambda e, f=f, tb=tb: e.tensor_copy(
                q2[:, f * T + tb * 512:f * T + (tb + 1) * 512], psum[:, :]))
            p.step("vector", lambda e, f=f, tb=tb: e.tensor_reduce(
                tmp1[:, :], q2[:, f * T + tb * 512:f * T + (tb + 1) * 512], axis=AX.X, op=OP.add))
            p.step("vector", lambda e, f=f: e.tensor_tensor(
                accs[:, f:f + 1], accs[:, f:f + 1], tmp1[:, :], op=OP.add))
            p.step("scalar", lambda e: e.activation(sq[:, :], psum[:, :], AF.Square))
            p.step("vector", lambda e: e.tensor_reduce(tmp1[:, :], sq[:, :], axis=AX.X, op=OP.add))
            p.step("vector", lambda e, f=f: e.tensor_tensor(
                accq[:, f:f + 1], accq[:, f:f + 1], tmp1[:, :], op=OP.add))
    p.step("scalar", lambda e: e.mul(accs[:, :], accs[:, :], 1.0 / T))
    p.step("scalar", lambda e: e.mul(accq[:, :], accq[:, :], 1.0 / T))
    p.step("scalar", lambda e: e.activation(g2[:, :], accs[:, :], AF.Square))
    p.step("vector", lambda e: e.tensor_tensor(accq[:, :], accq[:, :], g2[:, :], op=OP.subtract))
    p.step("vector", lambda e: e.tensor_scalar_add(accq[:, :], accq[:, :], 1e-5))
    p.step("scalar", lambda e: e.activation(accq[:, :], accq[:, :], AF.Sqrt))
    p.step("vector", lambda e: e.reciprocal(accq[:, :], accq[:, :]))
    p.step("vector", lambda e: e.tensor_tensor(g2[:, :], accq[:, :], gamsb[:, :], op=OP.mult))
    p.step("vector", lambda e: e.tensor_tensor(b2[:, :], accs[:, :], g2[:, :], op=OP.mult))
    p.step("vector", lambda e: e.tensor_tensor(b2[:, :], betsb[:, :], b2[:, :], op=OP.subtract))
    for f in range(2):
        for tb in range(T // 2048):
            p.step("vector", lambda e, f=f, tb=tb: e.tensor_scalar(
                q2[:, f * T + tb * 2048:f * T + (tb + 1) * 2048],
                q2[:, f * T + tb * 2048:f * T + (tb + 1) * 2048],
                g2[:, f:f + 1], b2[:, f:f + 1], op0=OP.mult, op1=OP.add))
    for tt in range(T // 128):
        for f, (svals, sidx, sflt) in enumerate(((sx, ixu, ixf), (sy, iyu, iyf))):
            p.step("tensor", lambda e, f=f, tt=tt: e.matmul(
                ps_sim[:, :], q2[:, f * T + tt * 128:f * T + (tt + 1) * 128],
                ksb[:, f * 256:(f + 1) * 256], start=True, stop=True))
            p.step("vector", lambda e: e.tensor_copy(simsb[:, :], ps_sim[:, :]))
            p.step("vector", lambda e, sv=svals: e.max(out=sv[:, :], in_=simsb[:, :]))
            p.step("vector", lambda e, sv=svals, si=sidx: e.max_index(si[:, :], sv[:, :], simsb[:, :]))
            p.step("vector", lambda e, si=sidx, sf=sflt: e.tensor_copy(sf[:, :], si[:, :]))
        ascv = asc[:, :].rearrange("p (i j) -> p i j", j=8)
        aidv = aid[:, :].rearrange("p (i j) -> p i j", j=8)
        for j in range(8):
            p.step("vector", lambda e, j=j: e.tensor_copy(ascv[:, :, j], sx[:, :]))
            p.step("vector", lambda e, j=j: e.tensor_copy(aidv[:, :, j], ixf[:, :]))
        p.step("vector", lambda e: e.tensor_scalar_mul(aid[:, :], aid[:, :], 256.0))
        for i in range(8):
            p.step("vector", lambda e, i=i: e.tensor_tensor(
                asc[:, i * 8:(i + 1) * 8], asc[:, i * 8:(i + 1) * 8], sy[:, :], op=OP.add))
            p.step("vector", lambda e, i=i: e.tensor_tensor(
                aid[:, i * 8:(i + 1) * 8], aid[:, i * 8:(i + 1) * 8], iyf[:, :], op=OP.add))
        p.step("vector", lambda e: e.max(out=s8[:, :], in_=asc[:, :]))
        p.step("scalar", lambda e: e.mul(tmp1[:, :], s8[:, 0:1], -1.0))
        p.step("scalar", lambda e: e.activation(e8[:, :], s8[:, :], AF.Exp, bias=tmp1[:, 0:1]))
        p.step("vector", lambda e: e.tensor_reduce(tmp1[:, :], e8[:, :], axis=AX.X, op=OP.add))
        p.step("vector", lambda e: e.reciprocal(tmp1[:, :], tmp1[:, :]))
        p.step("vector", lambda e: e.tensor_scalar(
            w8[:, :], e8[:, :], tmp1[:, 0:1], None, op0=OP.mult))
        for slot in range(8):
            p.step("vector", lambda e, s=slot: e.tensor_scalar(
                ohsb[:, :], asc[:, :], s8[:, s:s + 1], None, op0=OP.is_equal))
            p.step("vector", lambda e: e.tensor_tensor(
                junk[:, :], ohsb[:, :], aid[:, :], op=OP.mult))
            p.step("vector", lambda e, s=slot: e.tensor_reduce(
                idsel[:, s:s + 1], junk[:, :], axis=AX.X, op=OP.add))
        p.store(ids[tt * 128:(tt + 1) * 128, :], idsel[:, :])
        p.store(w8o[tt * 128:(tt + 1) * 128, :], w8[:, :])
    return p


def asb_slice(xsb, d):
    return xsb[:, d * 512:(d + 1) * 512]



def build_ffn0e():
    p = P("ffn0e")
    NEXP = 65536
    xn = p.din("xn", (TS, D))
    idf = p.din("idf", (TS, 64))
    w8 = p.din("w8", (TS, 64))
    demb = p.din("demb", (NEXP, D))
    uemb = p.din("uemb", (NEXP, D))
    y = p.dout("y", (TS, D))
    xt = p.sb("xt", (128, D))
    idt = p.sb("idt", (128, 64))
    wt = p.sb("wt", (128, 64))
    idu = p.sb("idu", (128, 64), U32)
    gd = p.sb("gd", (128, D))
    scr = p.sb("scr", (128, D))
    acc = p.sb("acc", (128, D))
    h = p.sb("h", (128, 64))
    h2 = p.sb("h2", (128, 64))
    u = p.sb("u", (128, 64))
    gw = p.sb("gw", (128, 64))
    c_gelu = float(np.sqrt(2.0 / np.pi))
    for tt in range(NTILE):
        sl = slice(tt * 128, (tt + 1) * 128)
        p.load(xt[:, :], xn[sl, :])
        p.load(idt[:, :], idf[sl, :])
        p.load(wt[:, :], w8[sl, :])
        p.step("vector", lambda e: e.tensor_copy(idu[:, :], idt[:, :]))
        for ee in range(64):
            p.step("gpsimd", lambda e, ee=ee: e.indirect_dma_start(
                out=gd[:, :], out_offset=None, in_=demb[0:128, :],
                in_offset=bass.IndirectOffsetOnAxis(ap=idu[:, ee:ee + 1], axis=0)), 16)
            p.step("vector", lambda e: e.tensor_tensor(
                scr[:, :], gd[:, :], xt[:, :], op=OP.mult))
            p.step("vector", lambda e, ee=ee: e.tensor_reduce(
                h[:, ee:ee + 1], scr[:, :], axis=AX.X, op=OP.add))
        p.step("vector", lambda e: e.tensor_tensor(h2[:, :], h[:, :], h[:, :], op=OP.mult))
        p.step("vector", lambda e: e.tensor_tensor(u[:, :], h2[:, :], h[:, :], op=OP.mult))
        p.step("vector", lambda e: e.tensor_scalar_mul(u[:, :], u[:, :], 0.044715))
        p.step("vector", lambda e: e.tensor_tensor(u[:, :], u[:, :], h[:, :], op=OP.add))
        p.step("scalar", lambda e: e.activation(u[:, :], u[:, :], AF.Tanh, scale=c_gelu))
        p.step("vector", lambda e: e.tensor_scalar(
            u[:, :], u[:, :], 1.0, 0.5, op0=OP.add, op1=OP.mult))
        p.step("vector", lambda e: e.tensor_tensor(u[:, :], u[:, :], h[:, :], op=OP.mult))
        p.step("vector", lambda e: e.tensor_tensor(gw[:, :], u[:, :], wt[:, :], op=OP.mult))
        p.step("vector", lambda e: e.memset(acc[:, :], 0.0))
        for ee in range(64):
            p.step("gpsimd", lambda e, ee=ee: e.indirect_dma_start(
                out=gd[:, :], out_offset=None, in_=uemb[0:128, :],
                in_offset=bass.IndirectOffsetOnAxis(ap=idu[:, ee:ee + 1], axis=0)), 16)
            p.step("vector", lambda e, ee=ee: e.scalar_tensor_tensor(
                out=acc[:, :], in0=gd[:, :], scalar=gw[:, ee:ee + 1], in1=acc[:, :],
                op0=OP.mult, op1=OP.add))
        p.store(y[sl, :], acc[:, :])
    return p



def shard_rows(x):
    return [np.ascontiguousarray(x[c * TS:(c + 1) * TS]) for c in range(NCORES)]


def shard_cols(xT):
    return [np.ascontiguousarray(xT[:, c * TS:(c + 1) * TS]) for c in range(NCORES)]


def cat_rows(parts):
    return np.concatenate(parts, axis=0)


def cat_cols(parts):
    return np.concatenate(parts, axis=1)


def bias_fm(b, nt):
    return np.ascontiguousarray(b.reshape(nt, 128).T)


def tile_bcast(v):
    return np.ascontiguousarray(np.broadcast_to(v.astype(np.float32), (128, v.shape[0])))


def run_lngb(x, g=None, b=None):
    p = get_prog("lngb", build_lngb)
    gt = tile_bcast(g if g is not None else np.ones(D, np.float32))
    bt = tile_bcast(b if b is not None else np.zeros(D, np.float32))
    res = launch(p, [{"xin": s, "g": gt, "b": bt} for s in shard_rows(x)])
    return cat_rows([r["y"] for r in res])


def run_add3(a, b, c):
    p = get_prog("add3", build_add3)
    res = launch(p, [{"a": sa, "b": sb, "c": sc} for sa, sb, sc in
                     zip(shard_rows(a), shard_rows(b), shard_rows(c))])
    return cat_rows([r["y"] for r in res])


def run_mm(xT, w, bias):
    K, N = w.shape
    name = "mm_%d_%d" % (K, N)
    p = get_prog(name, lambda: build_mm(name, K, N))
    bf = bias_fm(bias.astype(np.float32), N // 128)
    w = np.ascontiguousarray(w.astype(np.float32))
    res = launch(p, [{"aT": s, "w": w, "bias": bf} for s in shard_cols(xT)])
    return cat_cols([r["yT"] for r in res])


def run_phi(aT, bT):
    p = get_prog("phi", build_phi)
    res = launch(p, [{"a": sa, "b": sb} for sa, sb in zip(shard_cols(aT), shard_cols(bT))])
    return cat_cols([r["ya"] for r in res]), cat_cols([r["yb"] for r in res])


def pairs_list():
    return [(b, h) for b in range(4) for h in range(16)]


def pair_slices(xT, b, h):
    return np.ascontiguousarray(xT[h * 64:(h + 1) * 64, b * 4096:(b + 1) * 4096])


def transpose(x):
    return np.ascontiguousarray(x.T)


def shift_tokens_host(u):
    v = u.reshape(4, 4096, D)
    a = v[:, :, :512]
    s = v[:, :, 512:]
    s2 = np.concatenate([np.zeros((4, 1, 512), np.float32), s[:, :-1]], axis=1)
    return np.ascontiguousarray(np.concatenate([a, s2], -1).reshape(T, D))


def rotary_consts():
    DH = 64
    WINDOW = 256
    n = 4096
    half = np.arange(0, DH, 2, dtype=np.float32)
    inv_freq = (1.0 / 10000.0 ** (half / DH)).astype(np.float32)
    t = np.arange(n, dtype=np.float32)
    freqs = t[:, None] * inv_freq[None, :]
    freqs = np.concatenate([freqs, freqs], -1)
    base_scale = (half + 0.4 * DH) / (1.4 * DH)
    power = (t - n // 2) / (WINDOW // 2)
    scale = base_scale[None, :].astype(np.float32) ** power[:, None].astype(np.float32)
    scale = np.concatenate([scale, scale], -1).astype(np.float32)
    cos = np.cos(freqs).astype(np.float32)
    sin = np.sin(freqs).astype(np.float32)
    cs, ss = cos * scale, sin * scale
    ci, si = cos / scale, sin / scale
    out = []
    for m in (cs, ss, ci, si):
        m4 = np.tile(m, (4, 1))
        m4 = np.tile(m4, (1, 16))
        out.append(np.ascontiguousarray(m4.astype(np.float32)))
    return out


def latt_masks():
    W = 256
    NEG = -1e38
    m0 = np.zeros((256, 256), np.float32)
    for i in range(256):
        m0[i, i + 1:] = NEG
    mw = np.zeros((256, 512), np.float32)
    for i in range(256):
        for_j = np.arange(512)
        bad = (for_j < i) | (for_j > i + W)
        mw[i, bad] = NEG
    return m0, mw


_IDENT = np.eye(128, dtype=np.float32)
_TRIU_JT = np.triu(np.ones((128, 128), np.float32))



def kernel(**inp):
    f32 = np.float32
    X = np.ascontiguousarray(np.asarray(inp["L0_attn_x"], f32).reshape(T, D))

    xn0 = run_lngb(X)
    xn0T = transpose(xn0)

    qT = run_mm(xn0T, np.ascontiguousarray(inp["L0_attn_Wq"].T), inp["L0_attn_bq"])
    kT = run_mm(xn0T, np.ascontiguousarray(inp["L0_attn_Wk"].T), inp["L0_attn_bk"])
    vT = run_mm(xn0T, np.ascontiguousarray(inp["L0_attn_Wv"].T), inp["L0_attn_bv"])
    qT, kT = run_phi(qT, kT)

    pl = pairs_list()
    p_scan = get_prog("scan", build_scan)
    inmaps = []
    for c in range(NCORES):
        qs, ks, kts, vas = [], [], [], []
        for (b, h) in pl[c * 8:(c + 1) * 8]:
            qp = pair_slices(qT, b, h)
            kp = pair_slices(kT, b, h)
            vp = transpose(pair_slices(vT, b, h))
            qs.append(qp)
            ks.append(kp)
            kts.append(transpose(kp))
            vas.append(np.concatenate([vp, np.ones((4096, 1), f32)], 1))
        inmaps.append({"qT": np.stack(qs), "kT": np.stack(ks),
                       "ktok": np.stack(kts), "vaug": np.stack(vas),
                       "maskT": _TRIU_JT})
    res = launch(p_scan, inmaps)
    ctx0 = np.zeros((4, 16, 4096, 64), f32)
    for c in range(NCORES):
        for i, (b, h) in enumerate(pl[c * 8:(c + 1) * 8]):
            ctx0[b, h] = res[c]["o"][i]
    ctx0 = np.ascontiguousarray(ctx0.transpose(0, 2, 1, 3).reshape(T, D))
    a0T = run_mm(transpose(ctx0), np.ascontiguousarray(inp["L0_attn_Wo"].T), inp["L0_attn_bo"])
    a0 = transpose(a0T)

    p_h = get_prog("ffn0h", build_ffn0h)
    wq_full = np.asarray(inp["L0_ffn_w_q"], f32)
    gam_full = np.asarray(inp["L0_ffn_bn_gamma"], f32)
    bet_full = np.asarray(inp["L0_ffn_bn_beta"], f32)
    keys_full = np.asarray(inp["L0_ffn_keys"], f32)
    inmaps = []
    for hh in range(8):
        cols0 = slice(hh * 128, (hh + 1) * 128)
        cols1 = slice(1024 + hh * 128, 1024 + (hh + 1) * 128)
        wqh = np.concatenate([wq_full[:, cols0], wq_full[:, cols1]], 1)
        gam = np.stack([gam_full[cols0], gam_full[cols1]], 1)
        bet = np.stack([bet_full[cols0], bet_full[cols1]], 1)
        keysT = np.stack([np.ascontiguousarray(keys_full[hh, :, 0, :].T),
                          np.ascontiguousarray(keys_full[hh, :, 1, :].T)])
        inmaps.append({"xnT": xn0T, "wq": np.ascontiguousarray(wqh),
                       "gam": np.ascontiguousarray(gam), "bet": np.ascontiguousarray(bet),
                       "keysT": keysT})
    res = launch(p_h, inmaps)
    ids_all = np.stack([r["ids"] for r in res])
    w8_all = np.stack([r["w8"] for r in res])
    ids_tok = np.ascontiguousarray(ids_all.transpose(1, 0, 2).reshape(T, 64))
    w8_tok = np.ascontiguousarray(w8_all.transpose(1, 0, 2).reshape(T, 64))

    p_e = get_prog("ffn0e", build_ffn0e)
    demb = np.ascontiguousarray(np.asarray(inp["L0_ffn_down_embed"], f32))
    uemb = np.ascontiguousarray(np.asarray(inp["L0_ffn_up_embed"], f32))
    res = launch(p_e, [{"xn": sx, "idf": si, "w8": sw, "demb": demb, "uemb": uemb}
                       for sx, si, sw in zip(shard_rows(xn0), shard_rows(ids_tok),
                                             shard_rows(w8_tok))])
    f0 = cat_rows([r["y"] for r in res])

    X1 = run_add3(X, a0, f0)

    xn1 = run_lngb(X1)
    xn1T = transpose(xn1)

    q1T = run_mm(xn1T, np.ascontiguousarray(inp["L1_attn_Wq"].T), inp["L1_attn_bq"])
    k1T = run_mm(xn1T, np.ascontiguousarray(inp["L1_attn_Wk"].T), inp["L1_attn_bk"])
    v1T = run_mm(xn1T, np.ascontiguousarray(inp["L1_attn_Wv"].T), inp["L1_attn_bv"])

    p_att1 = get_prog("att1", build_att1)
    inmaps = []
    for c in range(NCORES):
        qs, ks, vas = [], [], []
        for (b, h) in pl[c * 8:(c + 1) * 8]:
            qs.append(pair_slices(q1T, b, h))
            ks.append(pair_slices(k1T, b, h))
            vp = transpose(pair_slices(v1T, b, h))
            vas.append(np.concatenate([vp, np.ones((4096, 1), f32)], 1))
        inmaps.append({"qT": np.stack(qs), "kT": np.stack(ks), "vaug": np.stack(vas),
                       "maskT": _TRIU_JT, "ident": _IDENT})
    res = launch(p_att1, inmaps)
    ctx1 = np.zeros((4, 16, 4096, 64), f32)
    for c in range(NCORES):
        for i, (b, h) in enumerate(pl[c * 8:(c + 1) * 8]):
            ctx1[b, h] = res[c]["o"][i].T
    ctx1 = np.ascontiguousarray(ctx1.transpose(0, 2, 1, 3).reshape(T, D))
    z1T = run_mm(transpose(ctx1), np.ascontiguousarray(inp["L1_attn_Wo"].T), inp["L1_attn_bo"])
    z1 = transpose(z1T)

    cs, ss, ci, si = rotary_consts()
    m0, mw = latt_masks()
    Wqkv = np.asarray(inp["L1_ffn_Wqkv"], f32)
    q_scale = np.asarray(inp["L1_ffn_q_scale"], f32)
    k_scale = np.asarray(inp["L1_ffn_k_scale"], f32)
    Wo6 = np.asarray(inp["L1_ffn_Wo"], f32)
    ln_g = np.asarray(inp["L1_ffn_ln_g"], f32)
    ln_b = np.asarray(inp["L1_ffn_ln_b"], f32)
    W1_6 = np.asarray(inp["L1_ffn_W1"], f32)
    b1_6 = np.asarray(inp["L1_ffn_b1"], f32)
    W2_6 = np.asarray(inp["L1_ffn_W2"], f32)
    b2_6 = np.asarray(inp["L1_ffn_b2"], f32)
    zerosD = np.zeros(D, f32)
    zerosTD = np.zeros((T, D), f32)

    u = xn1
    p_rot = get_prog("rot", build_rot)
    p_latt = get_prog("latt", build_latt)
    p_ffnsub = get_prog("ffnsub", build_ffnsub)
    for i in range(6):
        us = shift_tokens_host(u)
        qkvT = run_mm(transpose(us), Wqkv[i], np.zeros(3072, f32))
        q_tok = transpose(qkvT[0:1024])
        k_tok = transpose(qkvT[1024:2048])
        qs_t = tile_bcast(np.tile(q_scale[i], 16))
        ks_t = tile_bcast(np.tile(k_scale[i], 16))
        res = launch(p_rot, [
            {"q": sq_, "k": sk_, "qs": qs_t, "ks": ks_t,
             "cs": scs, "ss": sss, "ci": sci, "si": ssi}
            for sq_, sk_, scs, sss, sci, ssi in zip(
                shard_rows(q_tok), shard_rows(k_tok), shard_rows(cs), shard_rows(ss),
                shard_rows(ci), shard_rows(si))])
        qr = cat_rows([r["qr"] for r in res])
        kr = cat_rows([r["kr"] for r in res])
        qrT = transpose(qr)
        krT = transpose(kr)
        inmaps = []
        for c in range(NCORES):
            qs_, ks_, vs_ = [], [], []
            for (b, h) in pl[c * 8:(c + 1) * 8]:
                qs_.append(pair_slices(qrT, b, h))
                ks_.append(pair_slices(krT, b, h))
                vs_.append(transpose(
                    np.ascontiguousarray(qkvT[2048 + h * 64:2048 + (h + 1) * 64,
                                              b * 4096:(b + 1) * 4096])))
            inmaps.append({"qT": np.stack(qs_), "kT": np.stack(ks_), "vtok": np.stack(vs_),
                           "m0": m0, "mw": mw, "ident": _IDENT})
        res = launch(p_latt, inmaps)
        ctT = np.zeros((D, T), f32)
        for c in range(NCORES):
            for j, (b, h) in enumerate(pl[c * 8:(c + 1) * 8]):
                ctT[h * 64:(h + 1) * 64, b * 4096:(b + 1) * 4096] = res[c]["o"][j]
        zT = run_mm(ctT, Wo6[i], zerosD)
        u = run_add3(u, transpose(zT), zerosTD)

        us2 = shift_tokens_host(u)
        xnf = run_lngb(us2, ln_g[i], ln_b[i])
        W1i = W1_6[i]
        W1p = np.concatenate([W1i[:, :2730], np.zeros((D, 86), f32),
                              W1i[:, 2730:], np.zeros((D, 86), f32)], 1)
        b1i = b1_6[i]
        b1p = np.concatenate([b1i[:2730], np.zeros(86, f32),
                              b1i[2730:], np.zeros(86, f32)])
        W2p = np.concatenate([W2_6[i], np.zeros((86, D), f32)], 0)
        res = launch(p_ffnsub, [
            {"aT": s, "w1": np.ascontiguousarray(W1p), "b1": bias_fm(b1p, 44),
             "w2": np.ascontiguousarray(W2p), "b2": bias_fm(b2_6[i], 8)}
            for s in shard_cols(transpose(xnf))])
        y = transpose(cat_cols([r["yT"] for r in res]))
        u = run_add3(u, y, zerosTD)

    X2 = run_add3(X1, z1, u)
    return X2.reshape(4, 4096, D)


# version 03 (final)
# speedup vs baseline: 1.3058x; 1.3058x over previous
import numpy as np
from contextlib import ExitStack

import concourse.bass as bass
import concourse.mybir as mybir
from concourse.bass_utils import run_bass_kernel_spmd

F32 = mybir.dt.float32
U32 = mybir.dt.uint32
AF = mybir.ActivationFunctionType
OP = mybir.AluOpType
AX = mybir.AxisListType

NCORES = 8
T = 16384
D = 1024
TS = T // NCORES
NTILE = TS // 128



class P:
    def __init__(self, name):
        self.nc = bass.Bass("TRN2", target_bir_lowering=False)
        self.name = name
        self.stack = ExitStack()
        self.sem = self.stack.enter_context(self.nc.semaphore())
        self.steps = []
        self.outs = []
        self.built = False

    def din(self, name, shape, dtype=F32):
        return self.nc.dram_tensor(name, list(shape), dtype, kind="ExternalInput").ap()

    def dout(self, name, shape, dtype=F32):
        self.outs.append(name)
        return self.nc.dram_tensor(name, list(shape), dtype, kind="ExternalOutput").ap()

    def sb(self, name, shape, dtype=F32):
        return self.stack.enter_context(self.nc.sbuf_tensor(name, list(shape), dtype))

    def ps(self, name, shape, dtype=F32):
        return self.stack.enter_context(self.nc.psum_tensor(name, list(shape), dtype))

    def step(self, eng, fn, inc=1, wait=True, wait_steps=None):
        self.steps.append((eng, fn, inc, wait, wait_steps))
        return len(self.steps) - 1

    def load(self, out, in_):
        self.step("sync", lambda e, o=out, i=in_: e.dma_start(out=o, in_=i), inc=16)

    def store(self, out, in_):
        self.step("gpsimd", lambda e, o=out, i=in_: e.dma_start(out=o, in_=i), inc=16)

    def build(self):
        if self.built:
            return self
        nc = self.nc
        pre = []
        v = 0
        for (_, _, inc, _, _) in self.steps:
            pre.append(v)
            v += inc
        total = v
        done = [pre[i] + self.steps[i][2] for i in range(len(self.steps))]
        sem = self.sem
        steps = self.steps
        with nc.Block() as block:
            def mk(engname):
                def body(eng):
                    for i, (e, fn, inc, wait, wsteps) in enumerate(steps):
                        if e == engname:
                            if wsteps is not None:
                                for k in wsteps:
                                    eng.wait_ge(sem, done[k])
                            elif wait:
                                eng.wait_ge(sem, pre[i])
                            inst = fn(eng)
                            if inc:
                                inst.then_inc(sem, inc)
                    eng.wait_ge(sem, total)
                return body
            block.sync(mk("sync"))
            block.gpsimd(mk("gpsimd"))
            block.vector(mk("vector"))
            block.scalar(mk("scalar"))
            block.tensor(mk("tensor"))
        self.built = True
        return self


_progs = {}


def get_prog(name, builder):
    if name not in _progs:
        _progs[name] = builder().build()
    return _progs[name]


def launch(p, inmaps):
    res = run_bass_kernel_spmd(p.nc, inmaps, core_ids=list(range(NCORES)))
    return [{k: np.asarray(r[k]) for k in p.outs} for r in res.results]



def build_lngb():
    p = P("lngb")
    xin = p.din("xin", (TS, D))
    gin = p.din("g", (128, D))
    bin_ = p.din("b", (128, D))
    y = p.dout("y", (TS, D))
    xt = p.sb("xt", (128, D))
    gt = p.sb("gt", (128, D))
    bt = p.sb("bt", (128, D))
    sq = p.sb("sq", (128, D))
    yt = p.sb("yt", (128, D))
    s1 = p.sb("s1", (128, 1))
    s2 = p.sb("s2", (128, 1))
    s3 = p.sb("s3", (128, 1))
    p.load(gt[:, :], gin[:, :])
    p.load(bt[:, :], bin_[:, :])
    for t in range(NTILE):
        p.load(xt[:, :], xin[t * 128:(t + 1) * 128, :])
        p.step("vector", lambda e: e.tensor_reduce(s1[:, :], xt[:, :], axis=AX.X, op=OP.add))
        p.step("scalar", lambda e: e.mul(s2[:, :], s1[:, :], -1.0 / D))
        p.step("vector", lambda e: e.tensor_scalar(xt[:, :], xt[:, :], s2[:, 0:1], None, op0=OP.add))
        p.step("scalar", lambda e: e.activation(sq[:, :], xt[:, :], AF.Square))
        p.step("vector", lambda e: e.tensor_reduce(s1[:, :], sq[:, :], axis=AX.X, op=OP.add))
        p.step("scalar", lambda e: e.mul(s2[:, :], s1[:, :], 1.0 / D))
        p.step("vector", lambda e: e.tensor_scalar_add(s2[:, :], s2[:, :], 1e-5))
        p.step("scalar", lambda e: e.activation(s3[:, :], s2[:, :], AF.Sqrt))
        p.step("vector", lambda e: e.reciprocal(s1[:, :], s3[:, :]))
        p.step("vector", lambda e: e.tensor_scalar(xt[:, :], xt[:, :], s1[:, 0:1], None, op0=OP.mult))
        p.step("vector", lambda e: e.tensor_tensor(yt[:, :], xt[:, :], gt[:, :], op=OP.mult))
        p.step("vector", lambda e: e.tensor_tensor(yt[:, :], yt[:, :], bt[:, :], op=OP.add))
        p.store(y[t * 128:(t + 1) * 128, :], yt[:, :])
    return p



def build_add3():
    p = P("add3")
    a = p.din("a", (TS, D))
    b = p.din("b", (TS, D))
    c = p.din("c", (TS, D))
    y = p.dout("y", (TS, D))
    at = p.sb("at", (128, D))
    bt = p.sb("bt", (128, D))
    ct = p.sb("ct", (128, D))
    for t in range(NTILE):
        sl = slice(t * 128, (t + 1) * 128)
        p.load(at[:, :], a[sl, :])
        p.load(bt[:, :], b[sl, :])
        p.load(ct[:, :], c[sl, :])
        p.step("vector", lambda e: e.tensor_tensor(at[:, :], at[:, :], bt[:, :], op=OP.add))
        p.step("vector", lambda e: e.tensor_tensor(at[:, :], at[:, :], ct[:, :], op=OP.add))
        p.store(y[sl, :], at[:, :])
    return p



def build_mm(name, K, N):
    BF = mybir.dt.bfloat16
    p = P(name)
    kt = K // 128
    nt = N // 128
    a = p.din("aT", (K, TS))
    w = p.din("w", (K, N))
    bias = p.din("bias", (128, nt))
    y = p.dout("yT", (N, TS))
    wstg = p.sb("wstg", (128, N))
    wsb = p.sb("wsb", (128, kt * N), BF)
    astg = p.sb("astg", (128, 512))
    asb = p.sb("asb", (128, kt * 512), BF)
    bsb = p.sb("bsb", (128, nt))
    ysb = p.sb("ysb", (128, 512))
    psum = p.ps("psum", (128, 512))
    p.load(bsb[:, :], bias[:, :])
    for dt in range(kt):
        p.load(wstg[:, :], w[dt * 128:(dt + 1) * 128, :])
        p.step("vector", lambda e, d=dt: e.tensor_copy(wsb[:, d * N:(d + 1) * N], wstg[:, :]))
    for tb in range(TS // 512):
        for dt in range(kt):
            p.load(astg[:, :], a[dt * 128:(dt + 1) * 128, tb * 512:(tb + 1) * 512])
            p.step("vector", lambda e, d=dt: e.tensor_copy(asb[:, d * 512:(d + 1) * 512], astg[:, :]))
        for ot in range(nt):
            for dt in range(kt):
                p.step("tensor", lambda e, o=ot, d=dt: e.matmul(
                    psum[:, :],
                    wsb[:, d * N + o * 128:d * N + (o + 1) * 128],
                    asb[:, d * 512:(d + 1) * 512],
                    start=(d == 0), stop=(d == kt - 1)),
                    inc=(1 if dt == kt - 1 else 0), wait=(dt == 0))
            p.step("vector", lambda e, o=ot: e.tensor_scalar(
                ysb[:, :], psum[:, :], bsb[:, o:o + 1], None, op0=OP.add))
            p.store(y[ot * 128:(ot + 1) * 128, tb * 512:(tb + 1) * 512], ysb[:, :])
    return p



def build_phi():
    p = P("phi")
    a = p.din("a", (D, TS))
    b = p.din("b", (D, TS))
    ya = p.dout("ya", (D, TS))
    yb = p.dout("yb", (D, TS))
    xt = p.sb("xt", (128, TS))
    t1 = p.sb("t1", (128, TS))
    t2 = p.sb("t2", (128, TS))
    for (src, dst) in ((a, ya), (b, yb)):
        for ft in range(D // 128):
            sl = slice(ft * 128, (ft + 1) * 128)
            p.load(xt[:, :], src[sl, :])
            p.step("vector", lambda e: e.tensor_scalar_min(t1[:, :], xt[:, :], 0.0))
            p.step("scalar", lambda e: e.activation(t2[:, :], t1[:, :], AF.Exp))
            p.step("vector", lambda e: e.tensor_scalar_max(t1[:, :], xt[:, :], 0.0))
            p.step("vector", lambda e: e.tensor_tensor(xt[:, :], t1[:, :], t2[:, :], op=OP.add))
            p.store(dst[sl, :], xt[:, :])
    return p



def build_scan():
    p = P("scan")
    S = 4096
    C = 128
    NCH = S // C
    qT = p.din("qT", (8, 64, S))
    kT = p.din("kT", (8, 64, S))
    ktok = p.din("ktok", (8, S, 64))
    vaug = p.din("vaug", (8, S, 65))
    maskT = p.din("maskT", (128, 128))
    o = p.dout("o", (8, S, 64))
    qsb = p.sb("qsb", (64, S))
    ksb = p.sb("ksb", (64, S))
    ktsb = p.sb("ktsb", (128, NCH * 64))
    vsb = p.sb("vsb", (128, NCH * 65))
    msb = p.sb("msb", (128, 128))
    ssb = p.sb("ssb", (128, 128))
    Ssb = p.sb("Ssb", (64, 65))
    dsb = p.sb("dsb", (128, 1))
    osb = p.sb("osb", (128, 64))
    ps_s = p.ps("ps_s", (128, 128))
    ps_n = p.ps("ps_n", (128, 65))
    ps_S = p.ps("ps_S", (64, 65))
    p.load(msb[:, :], maskT[:, :])
    for pr in range(8):
        p.load(qsb[:, :], qT[pr, :, :])
        p.load(ksb[:, :], kT[pr, :, :])
        p.load(ktsb[:, :].rearrange("p (c d) -> p c d", d=64),
               ktok[pr, :, :].rearrange("(c p) d -> p c d", p=128))
        p.load(vsb[:, :].rearrange("p (c d) -> p c d", d=65),
               vaug[pr, :, :].rearrange("(c p) d -> p c d", p=128))
        p.step("vector", lambda e: e.memset(Ssb[:, :], 0.0))
        for c in range(NCH):
            cs = slice(c * 128, (c + 1) * 128)
            p.step("tensor", lambda e, cs=cs: e.matmul(
                ps_s[:, :], ksb[:, cs], qsb[:, cs], start=True, stop=True))
            p.step("vector", lambda e: e.tensor_tensor(ssb[:, :], ps_s[:, :], msb[:, :], op=OP.mult))
            p.step("tensor", lambda e, c=c: e.matmul(
                ps_n[:, :], ssb[:, :], vsb[:, c * 65:(c + 1) * 65], start=True, stop=False))
            p.step("tensor", lambda e, cs=cs: e.matmul(
                ps_n[:, :], qsb[:, cs], Ssb[:, :], start=False, stop=True))
            p.step("tensor", lambda e, c=c: e.matmul(
                ps_S[:, :], ktsb[:, c * 64:(c + 1) * 64], vsb[:, c * 65:(c + 1) * 65],
                start=(c == 0), stop=(c == NCH - 1)))
            p.step("vector", lambda e: e.tensor_copy(Ssb[:, :], ps_S[:, :]))
            p.step("vector", lambda e: e.reciprocal(dsb[:, :], ps_n[:, 64:65]))
            p.step("vector", lambda e: e.tensor_scalar(
                osb[:, :], ps_n[:, 0:64], dsb[:, 0:1], None, op0=OP.mult))
            p.store(o[pr, c * 128:(c + 1) * 128, :], osb[:, :])
    return p



def build_att1():
    p = P("att1")
    S = 4096
    qT = p.din("qT", (8, 64, S))
    kT = p.din("kT", (8, 64, S))
    vaug = p.din("vaug", (8, S, 65))
    maskT = p.din("maskT", (128, 128))
    ident = p.din("ident", (128, 128))
    BF = mybir.dt.bfloat16
    o = p.dout("o", (8, 64, S))
    qstg = p.sb("qstg", (64, S))
    vstg = p.sb("vstg", (128, 32 * 65))
    qsb = p.sb("qsb", (64, S), BF)
    ksb = p.sb("ksb", (64, S), BF)
    vsb = p.sb("vsb", (128, 32 * 65), BF)
    mstg = p.sb("mstg", (128, 128))
    msb = p.sb("msb", (128, 128), BF)
    isb = p.sb("isb", (128, 128))
    ssb = p.sb("ssb", (128, 512), BF)
    csb = p.sb("csb", (65, 512))
    one1 = p.sb("one1", (1, 128))
    dsb = p.sb("dsb", (1, 512))
    bsb = p.sb("bsb", (128, 512))
    osb = p.sb("osb", (64, 512))
    ps_s = p.ps("ps_s", (128, 512))
    ps_c = p.ps("ps_c", (65, 512))
    ps_b = p.ps("ps_b", (128, 512))
    p.load(mstg[:, :], maskT[:, :])
    p.step("vector", lambda e: e.tensor_copy(msb[:, :], mstg[:, :]))
    p.load(isb[:, :], ident[:, :])
    p.step("vector", lambda e: e.memset(one1[:, :], 1.0))
    for pr in range(8):
        p.load(qstg[:, :], qT[pr, :, :])
        p.step("vector", lambda e: e.tensor_copy(qsb[:, :], qstg[:, :]))
        p.load(qstg[:, :], kT[pr, :, :])
        p.step("vector", lambda e: e.tensor_copy(ksb[:, :], qstg[:, :]))
        p.load(vstg[:, :].rearrange("p (c d) -> p c d", d=65),
               vaug[pr, :, :].rearrange("(c p) d -> p c d", p=128))
        p.step("vector", lambda e: e.tensor_copy(vsb[:, :], vstg[:, :]))
        for g in range(8):
            gsl = slice(g * 512, (g + 1) * 512)
            for j in range(4 * g + 4):
                p.step("tensor", lambda e, j=j, gsl=gsl: e.matmul(
                    ps_s[:, :], ksb[:, j * 128:(j + 1) * 128], qsb[:, gsl],
                    start=True, stop=True))
                p.step("scalar", lambda e: e.activation(ssb[:, :], ps_s[:, :], AF.Relu, scale=0.125))
                if j >= 4 * g:
                    jj = j - 4 * g
                    if jj > 0:
                        p.step("vector", lambda e, jj=jj: e.memset(ssb[:, :jj * 128], 0.0))
                    p.step("vector", lambda e, jj=jj: e.tensor_tensor(
                        ssb[:, jj * 128:(jj + 1) * 128], ssb[:, jj * 128:(jj + 1) * 128],
                        msb[:, :], op=OP.mult))
                p.step("tensor", lambda e, j=j: e.matmul(
                    ps_c[:, :], vsb[:, j * 65:(j + 1) * 65], ssb[:, :],
                    start=(j == 0), stop=(j == 4 * g + 3)))
            p.step("vector", lambda e: e.tensor_copy(csb[:, :], ps_c[:, :]))
            p.step("vector", lambda e: e.tensor_scalar_add(dsb[:, :], csb[64:65, :], 1e-9))
            p.step("vector", lambda e: e.reciprocal(dsb[:, :], dsb[:, :]))
            p.step("tensor", lambda e: e.matmul(
                ps_b[:, :], one1[:, :], dsb[:, :], start=True, stop=True))
            p.step("vector", lambda e: e.tensor_copy(bsb[:, :], ps_b[:, :]))
            p.step("vector", lambda e: e.tensor_tensor(
                osb[:, :], csb[0:64, :], bsb[0:64, :], op=OP.mult))
            p.store(o[pr, :, gsl], osb[:, :])
    return p



def build_rot():
    p = P("rot")
    q = p.din("q", (TS, D))
    k = p.din("k", (TS, D))
    qs = p.din("qs", (128, D))
    ks = p.din("ks", (128, D))
    cs = p.din("cs", (TS, D))
    ss = p.din("ss", (TS, D))
    ci = p.din("ci", (TS, D))
    si = p.din("si", (TS, D))
    qr = p.dout("qr", (TS, D))
    kr = p.dout("kr", (TS, D))
    xt = p.sb("xt", (128, D))
    sct = p.sb("sct", (128, D))
    ct = p.sb("ct", (128, D))
    st = p.sb("st", (128, D))
    sq = p.sb("sq", (128, D))
    rh = p.sb("rh", (128, D))
    t1 = p.sb("t1", (128, D))
    nrm = p.sb("nrm", (128, 16))
    for (src, scl, cosin, dst, qscale) in ((q, qs, (cs, ss), qr, 8.0), (k, ks, (ci, si), kr, None)):
        p.load(sct[:, :], scl[:, :])
        for t in range(NTILE):
            sl = slice(t * 128, (t + 1) * 128)
            p.load(xt[:, :], src[sl, :])
            p.load(ct[:, :], cosin[0][sl, :])
            p.load(st[:, :], cosin[1][sl, :])
            p.step("scalar", lambda e: e.activation(sq[:, :], xt[:, :], AF.Square))
            for h in range(16):
                p.step("vector", lambda e, h=h: e.tensor_reduce(
                    nrm[:, h:h + 1], sq[:, h * 64:(h + 1) * 64], axis=AX.X, op=OP.add))
            p.step("scalar", lambda e: e.activation(nrm[:, :], nrm[:, :], AF.Sqrt))
            p.step("vector", lambda e: e.tensor_scalar_max(nrm[:, :], nrm[:, :], 1e-12))
            p.step("vector", lambda e: e.reciprocal(nrm[:, :], nrm[:, :]))
            for h in range(16):
                p.step("vector", lambda e, h=h: e.tensor_scalar(
                    xt[:, h * 64:(h + 1) * 64], xt[:, h * 64:(h + 1) * 64],
                    nrm[:, h:h + 1], None, op0=OP.mult))
            p.step("vector", lambda e: e.tensor_tensor(xt[:, :], xt[:, :], sct[:, :], op=OP.mult))
            xv = xt[:, :].rearrange("p (h u w) -> p h u w", h=16, u=2)
            rv = rh[:, :].rearrange("p (h u w) -> p h u w", h=16, u=2)
            p.step("scalar", lambda e: e.mul(rv[:, :, 0, :], xv[:, :, 1, :], -1.0))
            p.step("scalar", lambda e: e.copy(rv[:, :, 1, :], xv[:, :, 0, :]))
            p.step("vector", lambda e: e.tensor_tensor(t1[:, :], xt[:, :], ct[:, :], op=OP.mult))
            p.step("vector", lambda e: e.tensor_tensor(rh[:, :], rh[:, :], st[:, :], op=OP.mult))
            p.step("vector", lambda e: e.tensor_tensor(t1[:, :], t1[:, :], rh[:, :], op=OP.add))
            if qscale is not None:
                p.step("vector", lambda e, qs_=qscale: e.tensor_scalar_mul(t1[:, :], t1[:, :], qs_))
            p.store(dst[sl, :], t1[:, :])
    return p



def build_latt():
    p = P("latt")
    S = 4096
    W = 256
    NW = S // W
    qT = p.din("qT", (8, 64, S))
    kT = p.din("kT", (8, 64, S))
    vtok = p.din("vtok", (8, S, 64))
    m0 = p.din("m0", (256, 256))
    mw = p.din("mw", (256, 512))
    ident = p.din("ident", (128, 128))
    o = p.dout("o", (8, 64, S))
    qsb = p.sb("qsb", (64, S))
    ksb = p.sb("ksb", (64, S))
    vsb = p.sb("vsb", (128, 32 * 64))
    m0sb = p.sb("m0sb", (128, 2 * 256))
    mwsb = p.sb("mwsb", (128, 2 * 512))
    isb = p.sb("isb", (128, 128))
    es = p.sb("es", (128, 512))
    asb = p.sb("asb", (128, 512))
    atT = p.sb("atT", (128, 4 * 128))
    osb = p.sb("osb", (64, 128))
    r1 = p.sb("r1", (128, 1))
    r2 = p.sb("r2", (128, 1))
    ps_s = p.ps("ps_s", (128, 512))
    ps_t = p.ps("ps_t", (128, 128))
    ps_c = p.ps("ps_c", (64, 128))
    for s in range(2):
        p.load(m0sb[:, s * 256:(s + 1) * 256], m0[s * 128:(s + 1) * 128, :])
        p.load(mwsb[:, s * 512:(s + 1) * 512], mw[s * 128:(s + 1) * 128, :])
    p.load(isb[:, :], ident[:, :])
    for pr in range(8):
        p.load(qsb[:, :], qT[pr, :, :])
        p.load(ksb[:, :], kT[pr, :, :])
        p.load(vsb[:, :].rearrange("p (c d) -> p c d", d=64),
               vtok[pr, :, :].rearrange("(c p) d -> p c d", p=128))
        for w in range(NW):
            kbase = max(0, (w - 1) * W)
            kw = 256 if w == 0 else 512
            nkt = kw // 128
            for s in range(2):
                tq0 = w * W + s * 128
                msb = m0sb[:, s * 256:(s + 1) * 256] if w == 0 else mwsb[:, s * 512:(s + 1) * 512]
                p.step("tensor", lambda e, tq0=tq0, kbase=kbase, kw=kw: e.matmul(
                    ps_s[:, :kw], qsb[:, tq0:tq0 + 128], ksb[:, kbase:kbase + kw],
                    start=True, stop=True))
                p.step("vector", lambda e, kw=kw, msb=msb: e.tensor_tensor(
                    es[:, :kw], ps_s[:, :kw], msb, op=OP.add))
                p.step("vector", lambda e, kw=kw: e.tensor_reduce(
                    r1[:, :], es[:, :kw], axis=AX.X, op=OP.max))
                p.step("scalar", lambda e: e.mul(r2[:, :], r1[:, :], -1.0))
                p.step("scalar", lambda e, kw=kw: e.activation(
                    asb[:, :kw], es[:, :kw], AF.Exp, bias=r2[:, 0:1]))
                p.step("vector", lambda e, kw=kw: e.tensor_reduce(
                    r1[:, :], asb[:, :kw], axis=AX.X, op=OP.add))
                p.step("vector", lambda e: e.reciprocal(r2[:, :], r1[:, :]))
                p.step("vector", lambda e, kw=kw: e.tensor_scalar(
                    asb[:, :kw], asb[:, :kw], r2[:, 0:1], None, op0=OP.mult))
                for kt in range(nkt):
                    p.step("tensor", lambda e, kt=kt: e.transpose(
                        ps_t[:, :], asb[:, kt * 128:(kt + 1) * 128], isb[:, :]))
                    p.step("vector", lambda e, kt=kt: e.tensor_copy(
                        atT[:, kt * 128:(kt + 1) * 128], ps_t[:, :]))
                for kt in range(nkt):
                    vt = kbase // 128 + kt
                    p.step("tensor", lambda e, kt=kt, vt=vt, nkt=nkt: e.matmul(
                        ps_c[:, :], vsb[:, vt * 64:(vt + 1) * 64],
                        atT[:, kt * 128:(kt + 1) * 128],
                        start=(kt == 0), stop=(kt == nkt - 1)))
                p.step("vector", lambda e: e.tensor_copy(osb[:, :], ps_c[:, :]))
                p.store(o[pr, :, tq0:tq0 + 128], osb[:, :])
    return p



def build_ffnsub():
    BF = mybir.dt.bfloat16
    p = P("ffnsub")
    N1 = 5632
    NI = 2816
    a = p.din("aT", (D, TS))
    w1 = p.din("w1", (D, N1))
    b1 = p.din("b1", (128, N1 // 128))
    w2 = p.din("w2", (NI, D))
    b2 = p.din("b2", (128, D // 128))
    y = p.dout("yT", (D, TS))
    astg = p.sb("astg", (128, 512))
    asb = p.sb("asb", (128, 8 * 512), BF)
    w1stg = p.sb("w1stg", (128, 8 * 128))
    w1sb = p.sb("w1sb", (128, 8 * 128), BF)
    w2stg = p.sb("w2stg", (128, 22 * 128))
    w2sb = p.sb("w2sb", (128, 22 * 128), BF)
    b1sb = p.sb("b1sb", (128, N1 // 128))
    b2sb = p.sb("b2sb", (128, D // 128))
    agf = p.sb("agf", (128, 22 * 512))
    ag = p.sb("ag", (128, 22 * 512), BF)
    gt = p.sb("gt", (128, 512))
    ysb = p.sb("ysb", (128, 512))
    psum = p.ps("psum", (128, 512))
    ps2 = p.ps("ps2", (128, 512))
    p.load(b1sb[:, :], b1[:, :])
    p.load(b2sb[:, :], b2[:, :])
    nit = NI // 128
    for tb in range(TS // 512):
        for dt in range(8):
            p.load(astg[:, :], a[dt * 128:(dt + 1) * 128, tb * 512:(tb + 1) * 512])
            p.step("vector", lambda e, d=dt: e.tensor_copy(asb[:, d * 512:(d + 1) * 512], astg[:, :]))
        for ft in range(2 * nit):
            p.load(w1stg[:, :].rearrange("p (dt o) -> p dt o", o=128),
                   w1[:, ft * 128:(ft + 1) * 128].rearrange("(dt p) o -> p dt o", p=128))
            p.step("vector", lambda e: e.tensor_copy(w1sb[:, :], w1stg[:, :]))
            for dt in range(8):
                p.step("tensor", lambda e, d=dt: e.matmul(
                    psum[:, :], w1sb[:, d * 128:(d + 1) * 128], asb[:, d * 512:(d + 1) * 512],
                    start=(d == 0), stop=(d == 7)),
                    inc=(1 if dt == 7 else 0), wait=(dt == 0))
            if ft < nit:
                p.step("vector", lambda e, ft=ft: e.tensor_scalar(
                    agf[:, ft * 512:(ft + 1) * 512], psum[:, :], b1sb[:, ft:ft + 1],
                    None, op0=OP.add))
            else:
                p.step("scalar", lambda e, ft=ft: e.activation(
                    gt[:, :], psum[:, :], AF.Gelu, bias=b1sb[:, ft:ft + 1]))
                p.step("vector", lambda e, ft=ft: e.tensor_tensor(
                    agf[:, (ft - nit) * 512:(ft - nit + 1) * 512],
                    agf[:, (ft - nit) * 512:(ft - nit + 1) * 512], gt[:, :], op=OP.mult))
                p.step("vector", lambda e, ft=ft: e.tensor_copy(
                    ag[:, (ft - nit) * 512:(ft - nit + 1) * 512],
                    agf[:, (ft - nit) * 512:(ft - nit + 1) * 512]))
        for ot in range(8):
            p.load(w2stg[:, :].rearrange("p (it o) -> p it o", o=128),
                   w2[:, ot * 128:(ot + 1) * 128].rearrange("(it p) o -> p it o", p=128))
            p.step("vector", lambda e: e.tensor_copy(w2sb[:, :], w2stg[:, :]))
            for it in range(nit):
                p.step("tensor", lambda e, it=it: e.matmul(
                    ps2[:, :], w2sb[:, it * 128:(it + 1) * 128], ag[:, it * 512:(it + 1) * 512],
                    start=(it == 0), stop=(it == nit - 1)),
                    inc=(1 if it == nit - 1 else 0), wait=(it == 0))
            p.step("vector", lambda e, ot=ot: e.tensor_scalar(
                ysb[:, :], ps2[:, :], b2sb[:, ot:ot + 1], None, op0=OP.add))
            p.store(y[ot * 128:(ot + 1) * 128, tb * 512:(tb + 1) * 512], ysb[:, :])
    return p



def build_ffn0h():
    p = P("ffn0h")
    xnT = p.din("xnT", (D, T))
    wq = p.din("wq", (D, 256))
    gam = p.din("gam", (128, 2))
    bet = p.din("bet", (128, 2))
    keysT = p.din("keysT", (2, 128, 256))
    ids = p.dout("ids", (T, 8))
    w8o = p.dout("w8", (T, 8))
    q2 = p.sb("q2", (128, 2 * T))
    xsb = p.sb("xsb", (128, 8 * 512))
    wqsb = p.sb("wqsb", (128, 8 * 256))
    ksb = p.sb("ksb", (128, 512))
    sq = p.sb("sq", (128, 512))
    accs = p.sb("accs", (128, 2))
    accq = p.sb("accq", (128, 2))
    tmp1 = p.sb("tmp1", (128, 1))
    g2 = p.sb("g2", (128, 2))
    b2 = p.sb("b2", (128, 2))
    gamsb = p.sb("gamsb", (128, 2))
    betsb = p.sb("betsb", (128, 2))
    simsb = p.sb("simsb", (128, 256))
    sx = p.sb("sx", (128, 8))
    sy = p.sb("sy", (128, 8))
    ixu = p.sb("ixu", (128, 8), U32)
    iyu = p.sb("iyu", (128, 8), U32)
    ixf = p.sb("ixf", (128, 8))
    iyf = p.sb("iyf", (128, 8))
    asc = p.sb("asc", (128, 64))
    aid = p.sb("aid", (128, 64))
    s8 = p.sb("s8", (128, 8))
    e8 = p.sb("e8", (128, 8))
    w8 = p.sb("w8sb", (128, 8))
    idsel = p.sb("idsel", (128, 8))
    ohsb = p.sb("ohsb", (128, 64))
    junk = p.sb("junk", (128, 64))
    psum = p.ps("psum", (128, 512))
    ps_sim = p.ps("ps_sim", (128, 256))
    p.load(wqsb[:, :].rearrange("p (dt o) -> p dt o", o=256),
           wq[:, :].rearrange("(dt p) o -> p dt o", p=128))
    p.load(ksb[:, 0:256], keysT[0, :, :])
    p.load(ksb[:, 256:512], keysT[1, :, :])
    p.load(gamsb[:, :], gam[:, :])
    p.load(betsb[:, :], bet[:, :])
    p.step("vector", lambda e: e.memset(accs[:, :], 0.0))
    p.step("vector", lambda e: e.memset(accq[:, :], 0.0))
    for tb in range(T // 512):
        p.load(xsb[:, :].rearrange("p (dt t) -> p dt t", t=512),
               xnT[:, tb * 512:(tb + 1) * 512].rearrange("(dt p) t -> p dt t", p=128))
        for f in range(2):
            for dt in range(8):
                p.step("tensor", lambda e, f=f, d=dt: e.matmul(
                    psum[:, :], wqsb[:, d * 256 + f * 128:d * 256 + (f + 1) * 128],
                    asb_slice(xsb, d), start=(d == 0), stop=(d == 7)))
            p.step("vector", lambda e, f=f, tb=tb: e.tensor_copy(
                q2[:, f * T + tb * 512:f * T + (tb + 1) * 512], psum[:, :]))
            p.step("vector", lambda e, f=f, tb=tb: e.tensor_reduce(
                tmp1[:, :], q2[:, f * T + tb * 512:f * T + (tb + 1) * 512], axis=AX.X, op=OP.add))
            p.step("vector", lambda e, f=f: e.tensor_tensor(
                accs[:, f:f + 1], accs[:, f:f + 1], tmp1[:, :], op=OP.add))
            p.step("scalar", lambda e: e.activation(sq[:, :], psum[:, :], AF.Square))
            p.step("vector", lambda e: e.tensor_reduce(tmp1[:, :], sq[:, :], axis=AX.X, op=OP.add))
            p.step("vector", lambda e, f=f: e.tensor_tensor(
                accq[:, f:f + 1], accq[:, f:f + 1], tmp1[:, :], op=OP.add))
    p.step("scalar", lambda e: e.mul(accs[:, :], accs[:, :], 1.0 / T))
    p.step("scalar", lambda e: e.mul(accq[:, :], accq[:, :], 1.0 / T))
    p.step("scalar", lambda e: e.activation(g2[:, :], accs[:, :], AF.Square))
    p.step("vector", lambda e: e.tensor_tensor(accq[:, :], accq[:, :], g2[:, :], op=OP.subtract))
    p.step("vector", lambda e: e.tensor_scalar_add(accq[:, :], accq[:, :], 1e-5))
    p.step("scalar", lambda e: e.activation(accq[:, :], accq[:, :], AF.Sqrt))
    p.step("vector", lambda e: e.reciprocal(accq[:, :], accq[:, :]))
    p.step("vector", lambda e: e.tensor_tensor(g2[:, :], accq[:, :], gamsb[:, :], op=OP.mult))
    p.step("vector", lambda e: e.tensor_tensor(b2[:, :], accs[:, :], g2[:, :], op=OP.mult))
    p.step("vector", lambda e: e.tensor_tensor(b2[:, :], betsb[:, :], b2[:, :], op=OP.subtract))
    for f in range(2):
        for tb in range(T // 2048):
            p.step("vector", lambda e, f=f, tb=tb: e.tensor_scalar(
                q2[:, f * T + tb * 2048:f * T + (tb + 1) * 2048],
                q2[:, f * T + tb * 2048:f * T + (tb + 1) * 2048],
                g2[:, f:f + 1], b2[:, f:f + 1], op0=OP.mult, op1=OP.add))
    for tt in range(T // 128):
        for f, (svals, sidx, sflt) in enumerate(((sx, ixu, ixf), (sy, iyu, iyf))):
            p.step("tensor", lambda e, f=f, tt=tt: e.matmul(
                ps_sim[:, :], q2[:, f * T + tt * 128:f * T + (tt + 1) * 128],
                ksb[:, f * 256:(f + 1) * 256], start=True, stop=True))
            p.step("vector", lambda e: e.tensor_copy(simsb[:, :], ps_sim[:, :]))
            p.step("vector", lambda e, sv=svals: e.max(out=sv[:, :], in_=simsb[:, :]))
            p.step("vector", lambda e, sv=svals, si=sidx: e.max_index(si[:, :], sv[:, :], simsb[:, :]))
            p.step("vector", lambda e, si=sidx, sf=sflt: e.tensor_copy(sf[:, :], si[:, :]))
        ascv = asc[:, :].rearrange("p (i j) -> p i j", j=8)
        aidv = aid[:, :].rearrange("p (i j) -> p i j", j=8)
        for j in range(8):
            p.step("vector", lambda e, j=j: e.tensor_copy(ascv[:, :, j], sx[:, :]))
            p.step("vector", lambda e, j=j: e.tensor_copy(aidv[:, :, j], ixf[:, :]))
        p.step("vector", lambda e: e.tensor_scalar_mul(aid[:, :], aid[:, :], 256.0))
        for i in range(8):
            p.step("vector", lambda e, i=i: e.tensor_tensor(
                asc[:, i * 8:(i + 1) * 8], asc[:, i * 8:(i + 1) * 8], sy[:, :], op=OP.add))
            p.step("vector", lambda e, i=i: e.tensor_tensor(
                aid[:, i * 8:(i + 1) * 8], aid[:, i * 8:(i + 1) * 8], iyf[:, :], op=OP.add))
        p.step("vector", lambda e: e.max(out=s8[:, :], in_=asc[:, :]))
        p.step("scalar", lambda e: e.mul(tmp1[:, :], s8[:, 0:1], -1.0))
        p.step("scalar", lambda e: e.activation(e8[:, :], s8[:, :], AF.Exp, bias=tmp1[:, 0:1]))
        p.step("vector", lambda e: e.tensor_reduce(tmp1[:, :], e8[:, :], axis=AX.X, op=OP.add))
        p.step("vector", lambda e: e.reciprocal(tmp1[:, :], tmp1[:, :]))
        p.step("vector", lambda e: e.tensor_scalar(
            w8[:, :], e8[:, :], tmp1[:, 0:1], None, op0=OP.mult))
        for slot in range(8):
            p.step("vector", lambda e, s=slot: e.tensor_scalar(
                ohsb[:, :], asc[:, :], s8[:, s:s + 1], None, op0=OP.is_equal))
            p.step("vector", lambda e: e.tensor_tensor(
                junk[:, :], ohsb[:, :], aid[:, :], op=OP.mult))
            p.step("vector", lambda e, s=slot: e.tensor_reduce(
                idsel[:, s:s + 1], junk[:, :], axis=AX.X, op=OP.add))
        p.store(ids[tt * 128:(tt + 1) * 128, :], idsel[:, :])
        p.store(w8o[tt * 128:(tt + 1) * 128, :], w8[:, :])
    return p


def asb_slice(xsb, d):
    return xsb[:, d * 512:(d + 1) * 512]



def build_ffn0e():
    p = P("ffn0e")
    NEXP = 65536
    xn = p.din("xn", (TS, D))
    idf = p.din("idf", (TS, 64))
    w8 = p.din("w8", (TS, 64))
    demb = p.din("demb", (NEXP, D))
    uemb = p.din("uemb", (NEXP, D))
    y = p.dout("y", (TS, D))
    xt = p.sb("xt", (128, D))
    idt = p.sb("idt", (128, 64))
    wt = p.sb("wt", (128, 64))
    idu = p.sb("idu", (128, 64), U32)
    NB = 4
    gds = [p.sb("gd%d" % i, (128, D)) for i in range(NB)]
    scr = p.sb("scr", (128, D))
    acc = p.sb("acc", (128, D))
    h = p.sb("h", (128, 64))
    h2 = p.sb("h2", (128, 64))
    u = p.sb("u", (128, 64))
    gw = p.sb("gw", (128, 64))
    c_gelu = float(np.sqrt(2.0 / np.pi))
    for tt in range(NTILE):
        sl = slice(tt * 128, (tt + 1) * 128)
        p.load(xt[:, :], xn[sl, :])
        p.load(idt[:, :], idf[sl, :])
        p.load(wt[:, :], w8[sl, :])
        cast_i = p.step("vector", lambda e: e.tensor_copy(idu[:, :], idt[:, :]))

        def gather(dst, src, ee, **kw):
            return p.step("gpsimd", lambda e, ee=ee, dst=dst, src=src: e.indirect_dma_start(
                out=dst[:, :], out_offset=None, in_=src[0:128, :],
                in_offset=bass.IndirectOffsetOnAxis(ap=idu[:, ee:ee + 1], axis=0)),
                inc=16, **kw)

        g_idx = [gather(gds[ee % NB], demb, ee) for ee in range(NB)]
        m_idx = []
        for ee in range(64):
            wm = [g_idx[ee]] + ([m_idx[-1]] if m_idx else [])
            mi = p.step("vector", lambda e, ee=ee: e.tensor_tensor(
                scr[:, :], gds[ee % NB][:, :], xt[:, :], op=OP.mult), wait_steps=wm)
            ri = p.step("vector", lambda e, ee=ee: e.tensor_reduce(
                h[:, ee:ee + 1], scr[:, :], axis=AX.X, op=OP.add), wait_steps=[mi])
            m_idx.append(ri)
            if ee + NB < 64:
                g_idx.append(gather(gds[(ee + NB) % NB], demb, ee + NB,
                                    wait_steps=[mi]))
        p.step("vector", lambda e: e.tensor_tensor(h2[:, :], h[:, :], h[:, :], op=OP.mult))
        p.step("vector", lambda e: e.tensor_tensor(u[:, :], h2[:, :], h[:, :], op=OP.mult))
        p.step("vector", lambda e: e.tensor_scalar_mul(u[:, :], u[:, :], 0.044715))
        p.step("vector", lambda e: e.tensor_tensor(u[:, :], u[:, :], h[:, :], op=OP.add))
        p.step("scalar", lambda e: e.activation(u[:, :], u[:, :], AF.Tanh, scale=c_gelu))
        p.step("vector", lambda e: e.tensor_scalar(
            u[:, :], u[:, :], 1.0, 0.5, op0=OP.add, op1=OP.mult))
        p.step("vector", lambda e: e.tensor_tensor(u[:, :], u[:, :], h[:, :], op=OP.mult))
        p.step("vector", lambda e: e.tensor_tensor(gw[:, :], u[:, :], wt[:, :], op=OP.mult))
        p.step("vector", lambda e: e.memset(acc[:, :], 0.0))
        g_idx = [gather(gds[ee % NB], uemb, ee) for ee in range(NB)]
        s_idx = []
        for ee in range(64):
            wm = [g_idx[ee]] + ([s_idx[-1]] if s_idx else [])
            si = p.step("vector", lambda e, ee=ee: e.scalar_tensor_tensor(
                out=acc[:, :], in0=gds[ee % NB][:, :], scalar=gw[:, ee:ee + 1], in1=acc[:, :],
                op0=OP.mult, op1=OP.add), wait_steps=wm)
            s_idx.append(si)
            if ee + NB < 64:
                g_idx.append(gather(gds[(ee + NB) % NB], uemb, ee + NB,
                                    wait_steps=[si]))
        p.store(y[sl, :], acc[:, :])
    return p



def shard_rows(x):
    return [np.ascontiguousarray(x[c * TS:(c + 1) * TS]) for c in range(NCORES)]


def shard_cols(xT):
    return [np.ascontiguousarray(xT[:, c * TS:(c + 1) * TS]) for c in range(NCORES)]


def cat_rows(parts):
    return np.concatenate(parts, axis=0)


def cat_cols(parts):
    return np.concatenate(parts, axis=1)


def bias_fm(b, nt):
    return np.ascontiguousarray(b.reshape(nt, 128).T)


def tile_bcast(v):
    return np.ascontiguousarray(np.broadcast_to(v.astype(np.float32), (128, v.shape[0])))


def run_lngb(x, g=None, b=None):
    p = get_prog("lngb", build_lngb)
    gt = tile_bcast(g if g is not None else np.ones(D, np.float32))
    bt = tile_bcast(b if b is not None else np.zeros(D, np.float32))
    res = launch(p, [{"xin": s, "g": gt, "b": bt} for s in shard_rows(x)])
    return cat_rows([r["y"] for r in res])


def run_add3(a, b, c):
    p = get_prog("add3", build_add3)
    res = launch(p, [{"a": sa, "b": sb, "c": sc} for sa, sb, sc in
                     zip(shard_rows(a), shard_rows(b), shard_rows(c))])
    return cat_rows([r["y"] for r in res])


def run_mm(xT, w, bias):
    K, N = w.shape
    name = "mm_%d_%d" % (K, N)
    p = get_prog(name, lambda: build_mm(name, K, N))
    bf = bias_fm(bias.astype(np.float32), N // 128)
    w = np.ascontiguousarray(w.astype(np.float32))
    res = launch(p, [{"aT": s, "w": w, "bias": bf} for s in shard_cols(xT)])
    return cat_cols([r["yT"] for r in res])


def run_phi(aT, bT):
    p = get_prog("phi", build_phi)
    res = launch(p, [{"a": sa, "b": sb} for sa, sb in zip(shard_cols(aT), shard_cols(bT))])
    return cat_cols([r["ya"] for r in res]), cat_cols([r["yb"] for r in res])


def pairs_list():
    return [(b, h) for b in range(4) for h in range(16)]


def pair_slices(xT, b, h):
    return np.ascontiguousarray(xT[h * 64:(h + 1) * 64, b * 4096:(b + 1) * 4096])


def transpose(x):
    return np.ascontiguousarray(x.T)


def shift_tokens_host(u):
    v = u.reshape(4, 4096, D)
    a = v[:, :, :512]
    s = v[:, :, 512:]
    s2 = np.concatenate([np.zeros((4, 1, 512), np.float32), s[:, :-1]], axis=1)
    return np.ascontiguousarray(np.concatenate([a, s2], -1).reshape(T, D))


def rotary_consts():
    DH = 64
    WINDOW = 256
    n = 4096
    half = np.arange(0, DH, 2, dtype=np.float32)
    inv_freq = (1.0 / 10000.0 ** (half / DH)).astype(np.float32)
    t = np.arange(n, dtype=np.float32)
    freqs = t[:, None] * inv_freq[None, :]
    freqs = np.concatenate([freqs, freqs], -1)
    base_scale = (half + 0.4 * DH) / (1.4 * DH)
    power = (t - n // 2) / (WINDOW // 2)
    scale = base_scale[None, :].astype(np.float32) ** power[:, None].astype(np.float32)
    scale = np.concatenate([scale, scale], -1).astype(np.float32)
    cos = np.cos(freqs).astype(np.float32)
    sin = np.sin(freqs).astype(np.float32)
    cs, ss = cos * scale, sin * scale
    ci, si = cos / scale, sin / scale
    out = []
    for m in (cs, ss, ci, si):
        m4 = np.tile(m, (4, 1))
        m4 = np.tile(m4, (1, 16))
        out.append(np.ascontiguousarray(m4.astype(np.float32)))
    return out


def latt_masks():
    W = 256
    NEG = -1e38
    m0 = np.zeros((256, 256), np.float32)
    for i in range(256):
        m0[i, i + 1:] = NEG
    mw = np.zeros((256, 512), np.float32)
    for i in range(256):
        for_j = np.arange(512)
        bad = (for_j < i) | (for_j > i + W)
        mw[i, bad] = NEG
    return m0, mw


_IDENT = np.eye(128, dtype=np.float32)
_TRIU_JT = np.triu(np.ones((128, 128), np.float32))



def kernel(**inp):
    f32 = np.float32
    X = np.ascontiguousarray(np.asarray(inp["L0_attn_x"], f32).reshape(T, D))

    xn0 = run_lngb(X)
    xn0T = transpose(xn0)

    qT = run_mm(xn0T, np.ascontiguousarray(inp["L0_attn_Wq"].T), inp["L0_attn_bq"])
    kT = run_mm(xn0T, np.ascontiguousarray(inp["L0_attn_Wk"].T), inp["L0_attn_bk"])
    vT = run_mm(xn0T, np.ascontiguousarray(inp["L0_attn_Wv"].T), inp["L0_attn_bv"])
    qT, kT = run_phi(qT, kT)

    pl = pairs_list()
    p_scan = get_prog("scan", build_scan)
    inmaps = []
    for c in range(NCORES):
        qs, ks, kts, vas = [], [], [], []
        for (b, h) in pl[c * 8:(c + 1) * 8]:
            qp = pair_slices(qT, b, h)
            kp = pair_slices(kT, b, h)
            vp = transpose(pair_slices(vT, b, h))
            qs.append(qp)
            ks.append(kp)
            kts.append(transpose(kp))
            vas.append(np.concatenate([vp, np.ones((4096, 1), f32)], 1))
        inmaps.append({"qT": np.stack(qs), "kT": np.stack(ks),
                       "ktok": np.stack(kts), "vaug": np.stack(vas),
                       "maskT": _TRIU_JT})
    res = launch(p_scan, inmaps)
    ctx0 = np.zeros((4, 16, 4096, 64), f32)
    for c in range(NCORES):
        for i, (b, h) in enumerate(pl[c * 8:(c + 1) * 8]):
            ctx0[b, h] = res[c]["o"][i]
    ctx0 = np.ascontiguousarray(ctx0.transpose(0, 2, 1, 3).reshape(T, D))
    a0T = run_mm(transpose(ctx0), np.ascontiguousarray(inp["L0_attn_Wo"].T), inp["L0_attn_bo"])
    a0 = transpose(a0T)

    p_h = get_prog("ffn0h", build_ffn0h)
    wq_full = np.asarray(inp["L0_ffn_w_q"], f32)
    gam_full = np.asarray(inp["L0_ffn_bn_gamma"], f32)
    bet_full = np.asarray(inp["L0_ffn_bn_beta"], f32)
    keys_full = np.asarray(inp["L0_ffn_keys"], f32)
    inmaps = []
    for hh in range(8):
        cols0 = slice(hh * 128, (hh + 1) * 128)
        cols1 = slice(1024 + hh * 128, 1024 + (hh + 1) * 128)
        wqh = np.concatenate([wq_full[:, cols0], wq_full[:, cols1]], 1)
        gam = np.stack([gam_full[cols0], gam_full[cols1]], 1)
        bet = np.stack([bet_full[cols0], bet_full[cols1]], 1)
        keysT = np.stack([np.ascontiguousarray(keys_full[hh, :, 0, :].T),
                          np.ascontiguousarray(keys_full[hh, :, 1, :].T)])
        inmaps.append({"xnT": xn0T, "wq": np.ascontiguousarray(wqh),
                       "gam": np.ascontiguousarray(gam), "bet": np.ascontiguousarray(bet),
                       "keysT": keysT})
    res = launch(p_h, inmaps)
    ids_all = np.stack([r["ids"] for r in res])
    w8_all = np.stack([r["w8"] for r in res])
    ids_tok = np.ascontiguousarray(ids_all.transpose(1, 0, 2).reshape(T, 64))
    w8_tok = np.ascontiguousarray(w8_all.transpose(1, 0, 2).reshape(T, 64))

    p_e = get_prog("ffn0e", build_ffn0e)
    demb = np.ascontiguousarray(np.asarray(inp["L0_ffn_down_embed"], f32))
    uemb = np.ascontiguousarray(np.asarray(inp["L0_ffn_up_embed"], f32))
    res = launch(p_e, [{"xn": sx, "idf": si, "w8": sw, "demb": demb, "uemb": uemb}
                       for sx, si, sw in zip(shard_rows(xn0), shard_rows(ids_tok),
                                             shard_rows(w8_tok))])
    f0 = cat_rows([r["y"] for r in res])

    X1 = run_add3(X, a0, f0)

    xn1 = run_lngb(X1)
    xn1T = transpose(xn1)

    q1T = run_mm(xn1T, np.ascontiguousarray(inp["L1_attn_Wq"].T), inp["L1_attn_bq"])
    k1T = run_mm(xn1T, np.ascontiguousarray(inp["L1_attn_Wk"].T), inp["L1_attn_bk"])
    v1T = run_mm(xn1T, np.ascontiguousarray(inp["L1_attn_Wv"].T), inp["L1_attn_bv"])

    p_att1 = get_prog("att1", build_att1)
    inmaps = []
    for c in range(NCORES):
        qs, ks, vas = [], [], []
        for (b, h) in pl[c * 8:(c + 1) * 8]:
            qs.append(pair_slices(q1T, b, h))
            ks.append(pair_slices(k1T, b, h))
            vp = transpose(pair_slices(v1T, b, h))
            vas.append(np.concatenate([vp, np.ones((4096, 1), f32)], 1))
        inmaps.append({"qT": np.stack(qs), "kT": np.stack(ks), "vaug": np.stack(vas),
                       "maskT": _TRIU_JT, "ident": _IDENT})
    res = launch(p_att1, inmaps)
    ctx1 = np.zeros((4, 16, 4096, 64), f32)
    for c in range(NCORES):
        for i, (b, h) in enumerate(pl[c * 8:(c + 1) * 8]):
            ctx1[b, h] = res[c]["o"][i].T
    ctx1 = np.ascontiguousarray(ctx1.transpose(0, 2, 1, 3).reshape(T, D))
    z1T = run_mm(transpose(ctx1), np.ascontiguousarray(inp["L1_attn_Wo"].T), inp["L1_attn_bo"])
    z1 = transpose(z1T)

    cs, ss, ci, si = rotary_consts()
    m0, mw = latt_masks()
    Wqkv = np.asarray(inp["L1_ffn_Wqkv"], f32)
    q_scale = np.asarray(inp["L1_ffn_q_scale"], f32)
    k_scale = np.asarray(inp["L1_ffn_k_scale"], f32)
    Wo6 = np.asarray(inp["L1_ffn_Wo"], f32)
    ln_g = np.asarray(inp["L1_ffn_ln_g"], f32)
    ln_b = np.asarray(inp["L1_ffn_ln_b"], f32)
    W1_6 = np.asarray(inp["L1_ffn_W1"], f32)
    b1_6 = np.asarray(inp["L1_ffn_b1"], f32)
    W2_6 = np.asarray(inp["L1_ffn_W2"], f32)
    b2_6 = np.asarray(inp["L1_ffn_b2"], f32)
    zerosD = np.zeros(D, f32)
    zerosTD = np.zeros((T, D), f32)

    u = xn1
    p_rot = get_prog("rot", build_rot)
    p_latt = get_prog("latt", build_latt)
    p_ffnsub = get_prog("ffnsub", build_ffnsub)
    for i in range(6):
        us = shift_tokens_host(u)
        qkvT = run_mm(transpose(us), Wqkv[i], np.zeros(3072, f32))
        q_tok = transpose(qkvT[0:1024])
        k_tok = transpose(qkvT[1024:2048])
        qs_t = tile_bcast(np.tile(q_scale[i], 16))
        ks_t = tile_bcast(np.tile(k_scale[i], 16))
        res = launch(p_rot, [
            {"q": sq_, "k": sk_, "qs": qs_t, "ks": ks_t,
             "cs": scs, "ss": sss, "ci": sci, "si": ssi}
            for sq_, sk_, scs, sss, sci, ssi in zip(
                shard_rows(q_tok), shard_rows(k_tok), shard_rows(cs), shard_rows(ss),
                shard_rows(ci), shard_rows(si))])
        qr = cat_rows([r["qr"] for r in res])
        kr = cat_rows([r["kr"] for r in res])
        qrT = transpose(qr)
        krT = transpose(kr)
        inmaps = []
        for c in range(NCORES):
            qs_, ks_, vs_ = [], [], []
            for (b, h) in pl[c * 8:(c + 1) * 8]:
                qs_.append(pair_slices(qrT, b, h))
                ks_.append(pair_slices(krT, b, h))
                vs_.append(transpose(
                    np.ascontiguousarray(qkvT[2048 + h * 64:2048 + (h + 1) * 64,
                                              b * 4096:(b + 1) * 4096])))
            inmaps.append({"qT": np.stack(qs_), "kT": np.stack(ks_), "vtok": np.stack(vs_),
                           "m0": m0, "mw": mw, "ident": _IDENT})
        res = launch(p_latt, inmaps)
        ctT = np.zeros((D, T), f32)
        for c in range(NCORES):
            for j, (b, h) in enumerate(pl[c * 8:(c + 1) * 8]):
                ctT[h * 64:(h + 1) * 64, b * 4096:(b + 1) * 4096] = res[c]["o"][j]
        zT = run_mm(ctT, Wo6[i], zerosD)
        u = run_add3(u, transpose(zT), zerosTD)

        us2 = shift_tokens_host(u)
        xnf = run_lngb(us2, ln_g[i], ln_b[i])
        W1i = W1_6[i]
        W1p = np.concatenate([W1i[:, :2730], np.zeros((D, 86), f32),
                              W1i[:, 2730:], np.zeros((D, 86), f32)], 1)
        b1i = b1_6[i]
        b1p = np.concatenate([b1i[:2730], np.zeros(86, f32),
                              b1i[2730:], np.zeros(86, f32)])
        W2p = np.concatenate([W2_6[i], np.zeros((86, D), f32)], 0)
        res = launch(p_ffnsub, [
            {"aT": s, "w1": np.ascontiguousarray(W1p), "b1": bias_fm(b1p, 44),
             "w2": np.ascontiguousarray(W2p), "b2": bias_fm(b2_6[i], 8)}
            for s in shard_cols(transpose(xnf))])
        y = transpose(cat_cols([r["yT"] for r in res]))
        u = run_add3(u, y, zerosTD)

    X2 = run_add3(X1, z1, u)
    return X2.reshape(4, 4096, D)
